# version 11
# baseline (speedup 1.0000x reference)
import numpy as np
from contextlib import ExitStack
import concourse.bass as bass
import concourse.mybir as mybir
from concourse.bass_utils import run_bass_kernel_spmd

F32 = mybir.dt.float32
BF16 = mybir.dt.bfloat16
I32 = mybir.dt.int32
AF = mybir.ActivationFunctionType
ALU = mybir.AluOpType
AXX = mybir.AxisListType.X

T = 4096
D = 1024
DI = 2048
DP = 4256
HF = 4096
NH = 32
Q = 128
NCH = T // Q
TT = 512
NTT = T // TT
FP32_EPS = float(np.finfo(np.float32).eps)
LN_EPS = 1e-5
ENGINES = ("t", "v", "s", "g", "y")
P0_STAGE = 99


class Sem:
    def __init__(self, es, nc, name):
        self.h = es.enter_context(nc.semaphore(name))
        self.v = 0


class Sched:
    def __init__(self):
        self.st = {e: [] for e in ENGINES}

    def em(self, e, fn):
        self.st[e].append(fn)

    def w(self, e, sem, val):
        if val > 0:
            self.em(e, lambda eng, s=sem.h, v=val: eng.wait_ge(s, v))

    def run(self, block):
        st = self.st

        @block.tensor
        def _(t):
            for f in st["t"]:
                f(t)

        @block.vector
        def _(v):
            for f in st["v"]:
                f(v)

        @block.scalar
        def _(s):
            for f in st["s"]:
                f(s)

        @block.gpsimd
        def _(g):
            for f in st["g"]:
                f(g)

        @block.sync
        def _(y):
            for f in st["y"]:
                f(y)


def _early_exit(nc, in_maps):
    res = run_bass_kernel_spmd(nc, in_maps, core_ids=list(range(len(in_maps))))
    return res


def build_and_run(in_maps, upto="P3", dbg=False):
    nc = bass.Bass("TRN2", target_bir_lowering=False,
                  detect_race_conditions=False)
    with ExitStack() as es:

        def din(name, shape, dt=F32):
            return nc.dram_tensor(name, shape, dt, kind="ExternalInput").ap()

        u_d = din("u", [T, D])
        winT_d = din("winT", [D, DP])
        convw_d = din("convw", [128, 136])
        convb_d = din("convb", [128, 34])
        alog_d = din("alog", [NH, 1])
        dtb_d = din("dtb", [NH, 1])
        drow_d = din("drow", [1, DI])
        nwcol_d = din("nwcol", [128, 16])
        woT_d = din("woT", [DI, D])
        w1T_d = din("w1T", [D, HF])
        b1col_d = din("b1col", [128, 32])
        w2T_d = din("w2T", [HF, D])
        b2row_d = din("b2row", [1, D])
        out_d = nc.dram_tensor("out", [T, D], F32, kind="ExternalOutput").ap()

        skind = "ExternalOutput" if dbg else "Internal"

        def scr(name, shape, dt):
            return nc.dram_tensor(name, shape, dt, kind=skind).ap()

        z_scr = scr("z_scr", [16, 128, T], BF16)
        x_scr = scr("x_scr", [16, 128, T], BF16)
        bc_scr = scr("bc_scr", [128, T], F32)
        dt_scr = scr("dt_scr", [NH, T], F32)
        xmid_scr = scr("xmid_scr", [T, D], F32)
        h_scr = scr("h_scr", [32, 128, T], F32)
        w1q_scr = scr("w1q_scr", [32, 128, 8, 128], BF16)
        w2q_scr = scr("w2q_scr", [32, 128, D], BF16)
        sdbg = scr("sdbg", [128, 64], F32)
        cc1_in = nc.dram_tensor("cc1_in", [1, 2], F32, kind="Internal")
        cc1_out = nc.dram_tensor("cc1_out", [1, 2], F32, kind="Internal")
        cc2_in = nc.dram_tensor("cc2_in", [1, 2], F32, kind="Internal")
        cc2_out = nc.dram_tensor("cc2_out", [1, 2], F32, kind="Internal")

        s_pe = Sem(es, nc, "s_pe")
        s_dv = Sem(es, nc, "s_dv")
        s_ac = Sem(es, nc, "s_ac")
        s_gp = Sem(es, nc, "s_gp")
        s_sy = Sem(es, nc, "s_sy")
        d_u = Sem(es, nc, "d_u")
        d_w = Sem(es, nc, "d_w")
        d_sc = Sem(es, nc, "d_sc")
        d_mv = Sem(es, nc, "d_mv")
        d_cc = Sem(es, nc, "d_cc")
        d_out = Sem(es, nc, "d_out")
        ESEM = {"t": s_pe, "v": s_dv, "s": s_ac, "g": s_gp, "y": s_sy}

        def sb(name, shape, dt, stack=None):
            return (stack or es).enter_context(
                nc.sbuf_tensor(name, shape, dt))

        ones_f = sb("ones_f", [128, 128], F32)
        tril_f = sb("tril_f", [128, 128], F32)
        ident_b = sb("ident_b", [128, 128], BF16)
        identf = sb("identf", [128, 128], F32)
        onecol_b = sb("onecol_b", [128, 1], BF16)
        convw_s = sb("convw_s", [128, 136], F32)
        convb_s = sb("convb_s", [128, 34], F32)
        alog_s = sb("alog_s", [NH, 1], F32)
        dtb_s = sb("dtb_s", [NH, 1], F32)
        nwcol_s = sb("nwcol_s", [128, 16], F32)
        b1col_s = sb("b1col_s", [128, 32], F32)
        b2row_s = sb("b2row_s", [1, D], F32)
        sc_s = sb("sc_s", [1, 64], F32)
        colsum1_s = sb("colsum1_s", [128, 32], F32)
        colsum2_s = sb("colsum2_s", [1, D], F32)
        rw_s = sb("rw_s", [128, 2], F32)
        m2_s = sb("m2_s", [128, NCH], F32)
        inv2_s = sb("inv2_s", [128, NCH], F32)
        mmt1 = sb("mmt1", [128, 64], F32)
        mmt2 = sb("mmt2", [128, 64], F32)
        q1s_s = sb("q1s_s", [128, NCH], F32)
        q1b_s = sb("q1b_s", [128, NCH], F32)
        b1mod_s = sb("b1mod_s", [128, 32], F32)
        eff_s = sb("eff_s", [128, 8], F32)
        b2mod_s = sb("b2mod_s", [1, D], F32)
        b2bc = sb("b2bc", [128, D], F32)
        clip_lo = sb("clip_lo", [128, 1], F32)
        clip_hi = sb("clip_hi", [128, 1], F32)
        cst_t = sb("cst_t", [128, 2], F32)
        nc.const_aps.register(F32, LN_EPS, cst_t[:, 0:1])
        nc.const_aps.register(F32, FP32_EPS, cst_t[:, 1:2])
        mmrow = sb("mmrow", [1, 2], F32)
        ccr = sb("ccr", [1, 2], F32)
        ms = es.enter_context(ExitStack())
        dbc_b = sb("dbc_b", [128, DI], BF16, ms)
        state0 = sb("state0", [64, 1024], F32, ms)
        state1 = sb("state1", [64, 1024], F32, ms)
        state0b = sb("state0b", [64, 1024], BF16, ms)
        state1b = sb("state1b", [64, 1024], BF16, ms)

        psum = es.enter_context(nc.psum_tensor("psum", [128, 4096], F32))

        def pk(c0, w=512):
            return psum[:, c0:c0 + w]

        def run_phase(sc):
            blk = nc.Block()
            blk.__enter__()
            sc.run(blk)
            blk.__exit__(None, None, None)

        def barrier(sc):
            marks = {e: ESEM[e].v for e in ENGINES}
            dmarks = [(d_u, d_u.v), (d_w, d_w.v), (d_sc, d_sc.v),
                      (d_mv, d_mv.v), (d_cc, d_cc.v)]
            for e in ENGINES:
                for e2 in ENGINES:
                    if e2 != e:
                        sc.w(e, ESEM[e2], marks[e2])
                for dm, dv in dmarks:
                    sc.w(e, dm, dv)

        with ExitStack() as winscope:
            win_bf = sb("win_bf", [128, 8 * DP], BF16, winscope)

            with ExitStack() as p0s:
                drow_s = sb("drow_s", [1, DI], F32, p0s)
                sc = Sched()

                def g_consts(g):
                    g.memset(ones_f[:], 1.0)
                    g.affine_select(tril_f[:], ones_f[:], [[1, 128]],
                                    ALU.is_ge, 0.0, base=0,
                                    channel_multiplier=-1)
                    g.affine_select(identf[:], ones_f[:], [[1, 128]],
                                    ALU.is_equal, 0.0, base=0,
                                    channel_multiplier=-1)
                    g.memset(clip_lo[:], -128.0)
                    g.memset(clip_hi[:], 127.0)
                    g.memset(state0[:], 0.0)
                    g.memset(state1[:], 0.0)
                    g.memset(state0b[:], 0.0)
                    g.memset(state1b[:], 0.0)
                    g.memset(sc_s[:], 0.0)
                    g.memset(colsum1_s[:], 0.0)
                    g.memset(cst_t[:, 0:1], LN_EPS)
                    _li = g.memset(cst_t[:, 1:2], FP32_EPS)
                    _li.then_inc(s_gp.h, 1)
                sc.em("g", g_consts)
                s_gp.v += 1
                gp_c = s_gp.v
                sc.w("v", s_gp, gp_c)

                def v_idc(v):
                    v.tensor_copy(ident_b[:], identf[:])
                    _li = v.tensor_copy(onecol_b[:], ones_f[:, 0:1])
                    _li.then_inc(s_dv.h, 1)
                sc.em("v", v_idc)
                s_dv.v += 1

                def y_small(y):
                    for dst, src in ((convw_s, convw_d), (convb_s, convb_d),
                                     (alog_s, alog_d), (dtb_s, dtb_d),
                                     (nwcol_s, nwcol_d), (b1col_s, b1col_d),
                                     (b2row_s, b2row_d), (drow_s, drow_d)):
                        y.dma_start(out=dst[:], in_=src).then_inc(d_u.h, 16)
                sc.em("y", y_small)
                d_u.v += 128
                if P0_STAGE < 2:
                    barrier(sc)
                    run_phase(sc)
                    raise SystemExit(_early_exit(nc, in_maps))
                sc.w("t", d_u, d_u.v)
                sc.w("t", s_gp, gp_c)

                def t_dbc(t):
                    for j in range(4):
                        _li = t.matmul(pk(j * 512), ones_f[0:1, :],
                                 drow_s[:, j * 512:(j + 1) * 512],
                                 start=True, stop=True)
                    _li.then_inc(s_pe.h, 1)
                sc.em("t", t_dbc)
                s_pe.v += 1
                sc.w("v", s_pe, s_pe.v)
                sc.em("v", lambda v: v.tensor_copy(
                    dbc_b[:], psum[:, 0:2048]).then_inc(s_dv.h, 1))
                s_dv.v += 1
                dv_dbc = s_dv.v

                if P0_STAGE < 3:
                    barrier(sc)
                    run_phase(sc)
                    raise SystemExit(_early_exit(nc, in_maps))
                for dtl in range(8):
                    sc.em("g", lambda g, dtl=dtl: g.dma_start(
                        out=win_bf[:, dtl * DP:(dtl + 1) * DP],
                        in_=winT_d[dtl * 128:(dtl + 1) * 128, :])
                        .then_inc(d_mv.h, 16))
                    d_mv.v += 16

                WW = 2048
                wrow = [sb(f"wrow{k}", [128, WW], F32, p0s) for k in range(2)]
                wsc = [sb(f"wsc{k}", [128, WW], F32, p0s) for k in range(2)]
                wqi = [sb(f"wqi{k}", [128, WW], I32, p0s) for k in range(2)]
                wqb = [sb(f"wqb{k}", [128, WW], BF16, p0s) for k in range(2)]
                maxc = sb("maxc", [128, 48], F32, p0s)

                pieces = []
                for r in range(8):
                    for hf in range(2):
                        pieces.append(("w1", r, hf,
                                       w1T_d[r * 128:(r + 1) * 128,
                                             hf * WW:(hf + 1) * WW], WW))
                for r in range(32):
                    pieces.append(("w2", r, 0,
                                   w2T_d[r * 128:(r + 1) * 128, :], D))

                if P0_STAGE < 4:
                    barrier(sc)
                    run_phase(sc)
                    raise SystemExit(_early_exit(nc, in_maps))
                dvm = {}
                for idx, (wn, r, hf, src, width) in enumerate(pieces):
                    b = idx % 2
                    if idx >= 2:
                        sc.w("y", s_dv, dvm[idx - 2])
                    sc.em("y", lambda y, b=b, src=src, width=width:
                          y.dma_start(out=wrow[b][:, 0:width], in_=src)
                          .then_inc(d_w.h, 16))
                    d_w.v += 16
                    sc.w("v", d_w, d_w.v)
                    col = idx if wn == "w1" else 16 + r
                    sc.em("v", lambda v, b=b, width=width, col=col:
                          v.reduce_max(maxc[:, col:col + 1],
                                       wrow[b][:, 0:width], AXX,
                                       apply_absolute_value=True)
                          .then_inc(s_dv.h, 1))
                    s_dv.v += 1
                    dvm[idx] = s_dv.v

                def v_comb(v):
                    v.reduce_max(maxc[:, 0:1], maxc[:, 0:16], AXX)
                    _li = v.reduce_max(maxc[:, 1:2], maxc[:, 16:48], AXX)
                    _li.then_inc(s_dv.h, 1)
                sc.em("v", v_comb)
                s_dv.v += 1
                sc.w("t", s_dv, s_dv.v)
                sc.w("t", s_dv, dv_dbc)
                def t_trm(t):
                    t.transpose(psum[0:1, 0:128], maxc[:, 0:1], identf[:])
                    _li = t.transpose(psum[0:1, 128:256], maxc[:, 1:2], identf[:])
                    _li.then_inc(s_pe.h, 1)
                sc.em("t", t_trm)
                s_pe.v += 1
                sc.w("v", s_pe, s_pe.v)

                def v_sw_a(v):
                    v.reduce_max(sc_s[:, 0:1], psum[0:1, 0:128], AXX)
                    v.reduce_max(sc_s[:, 1:2], psum[0:1, 128:256], AXX)\
                        .then_inc(s_dv.h, 1)
                sc.em("v", v_sw_a)
                s_dv.v += 1
                sc.w("v", s_dv, s_dv.v)
                sc.em("v", lambda v: v.tensor_scalar(
                    sc_s[:, 2:4], sc_s[:, 0:2], 1.0 / 127.0, 1e-8,
                    ALU.mult, ALU.max).then_inc(s_dv.h, 1))
                s_dv.v += 1
                sc.w("v", s_dv, s_dv.v)
                sc.em("v", lambda v: v.reciprocal(
                    sc_s[:, 4:6], sc_s[:, 2:4]).then_inc(s_dv.h, 1))
                s_dv.v += 1
                sc.w("t", s_dv, s_dv.v)
                sc.em("t", lambda t: t.matmul(
                    psum[:, 0:2], ones_f[0:1, :], sc_s[0:1, 4:6],
                    start=True, stop=True).then_inc(s_pe.h, 1))
                s_pe.v += 1
                sc.w("v", s_pe, s_pe.v)
                sc.em("v", lambda v: v.tensor_copy(rw_s[:], psum[:, 0:2])
                      .then_inc(s_dv.h, 1))
                s_dv.v += 1
                dv_rw = s_dv.v
                if dbg:
                    sc.w("g", s_dv, dv_rw)
                    sc.em("g", lambda g: g.dma_start(
                        out=sdbg[0:1, :], in_=sc_s[:]).then_inc(d_mv.h, 16))
                    sc.em("g", lambda g: g.dma_start(
                        out=sdbg[1:3, 0:2], in_=rw_s[0:2, :])
                        .then_inc(d_mv.h, 16))
                    sc.em("g", lambda g: g.dma_start(
                        out=sdbg[4:5, 0:48], in_=maxc[0:1, :])
                        .then_inc(d_mv.h, 16))
                    d_mv.v += 48

                if P0_STAGE < 5:
                    barrier(sc)
                    run_phase(sc)
                    raise SystemExit(_early_exit(nc, in_maps))
                dvq = {}
                peq = {}
                scw = {}
                csadd = {}
                for idx, (wn, r, hf, src, width) in enumerate(pieces):
                    b = idx % 2
                    if idx >= 2:
                        sc.w("y", s_dv, dvq[idx - 2])
                    sc.em("y", lambda y, b=b, src=src, width=width:
                          y.dma_start(out=wrow[b][:, 0:width], in_=src)
                          .then_inc(d_w.h, 16))
                    d_w.v += 16
                    sc.w("s", d_w, d_w.v)
                    if idx == 0:
                        sc.w("s", s_dv, dv_rw)
                    if idx >= 2:
                        sc.w("s", s_dv, dvq[idx - 2])
                    rwc = rw_s[:, 0:1] if wn == "w1" else rw_s[:, 1:2]
                    sc.em("s", lambda s, b=b, width=width, rwc=rwc:
                          s.activation(wsc[b][:, 0:width],
                                       wrow[b][:, 0:width], AF.Copy,
                                       scale=rwc).then_inc(s_ac.h, 1))
                    s_ac.v += 1
                    sc.w("v", s_ac, s_ac.v)
                    if idx >= 2:
                        sc.w("v", d_sc, scw[idx - 2])
                        sc.w("v", s_pe, peq[idx - 2])

                    def v_q(v, b=b, width=width):
                        v.tensor_copy(wqi[b][:, 0:width], wsc[b][:, 0:width])
                        _li = v.tensor_copy(wqb[b][:, 0:width], wqi[b][:, 0:width])
                        _li.then_inc(s_dv.h, 1)
                    sc.em("v", v_q)
                    s_dv.v += 1
                    dvq[idx] = s_dv.v
                    sc.w("g", s_dv, dvq[idx])
                    if wn == "w1":
                        def g_w1(g, b=b, r=r, hf=hf):
                            dst = w1q_scr[hf * 16:(hf + 1) * 16, :, r, :]
                            g.dma_start(
                                out=dst.transpose([1, 0, 2]),
                                in_=wqb[b][:].rearrange(
                                    "p (j c) -> p j c", c=128))\
                                .then_inc(d_sc.h, 16)
                        sc.em("g", g_w1)
                    else:
                        sc.em("g", lambda g, b=b, r=r: g.dma_start(
                            out=w2q_scr[r], in_=wqb[b][:, 0:D])
                            .then_inc(d_sc.h, 16))
                    d_sc.v += 16
                    scw[idx] = d_sc.v
                    sc.w("t", s_dv, dvq[idx])
                    if wn == "w1":
                        if idx >= 1:
                            sc.w("t", s_dv, csadd.get(idx - 1, 0))

                        def t_cs1(t, b=b, hf=hf):
                            for j in range(16):
                                _li = t.matmul(psum[:, 1024 + j:1025 + j],
                                         wqb[b][:, j * 128:(j + 1) * 128],
                                         onecol_b[:],
                                         start=True, stop=True)
                            _li.then_inc(s_pe.h, 1)
                        sc.em("t", t_cs1)
                        s_pe.v += 1
                        peq[idx] = s_pe.v
                        sc.w("v", s_pe, peq[idx])

                        def v_csadd(v, hf=hf):
                            _li = v.tensor_tensor(
                                colsum1_s[:, hf * 16:(hf + 1) * 16],
                                colsum1_s[:, hf * 16:(hf + 1) * 16],
                                psum[:, 1024:1040], ALU.add)
                            _li.then_inc(s_dv.h, 1)
                        sc.em("v", v_csadd)
                        s_dv.v += 1
                        csadd[idx] = s_dv.v
                        continue
                    else:
                        def t_cs2(t, b=b, r=r):
                            for j in range(2):
                                _li = t.matmul(psum[0:1, 1536 + j * 512:
                                              2048 + j * 512],
                                         onecol_b[:],
                                         wqb[b][:, j * 512:(j + 1) * 512],
                                         start=(r == 0), stop=(r == 31))
                            _li.then_inc(s_pe.h, 1)
                        sc.em("t", t_cs2)
                    s_pe.v += 1
                    peq[idx] = s_pe.v
                sc.w("v", s_pe, s_pe.v)

                def v_cs(v):
                    _li = v.tensor_copy(colsum2_s[:], psum[0:1, 1536:2560])
                    _li.then_inc(s_dv.h, 1)
                sc.em("v", v_cs)
                s_dv.v += 1

                barrier(sc)
                run_phase(sc)

            if upto in ("P1a", "P1c", "P2", "P3"):
                with ExitStack() as ps:
                    u_ring = [sb(f"u_ring{k}", [128, 4 * D], F32, ps)
                              for k in range(2)]
                    lnx_tm = [sb(f"lnx_tm{k}", [128, 4 * D], BF16, ps)
                              for k in range(2)]
                    lnx_fm = sb("lnx_fm", [128, 8 * TT], BF16, ps)
                    lnst = sb("lnst", [128, 16], F32, ps)
                    bn12 = sb("bn12", [128, 48], F32, ps)
                    mv2 = sb("mv2", [128, 8], F32, ps)
                    zx_ring = [sb(f"zxr{k}", [128, 515], BF16, ps)
                               for k in range(4)]
                    bcbuf = sb("bcbuf", [128, 515], F32, ps)
                    dtbuf = sb("dtbuf", [NH, 515], F32, ps)
                    hist_b = sb("hist_b", [128, 96], BF16, ps)
                    hist_f = sb("hist_f", [128, 6], F32, ps)
                    cbuf = [sb(f"cbuf{k}", [128, TT], BF16, ps)
                            for k in range(4)]
                    bcc = sb("bcc", [128, TT], F32, ps)
                    dtc = sb("dtc", [NH, TT], F32, ps)

                    sc = Sched()

                    def g_hist(g):
                        g.memset(hist_b[:], 0.0)
                        g.memset(hist_f[:], 0.0).then_inc(s_gp.h, 1)
                    sc.em("g", g_hist)
                    s_gp.v += 1
                    gp_h0 = s_gp.v
                    sc.w("s", s_gp, gp_h0)
                    sc.w("v", s_gp, gp_h0)

                    u_mark = {}
                    ac_lnx = {}
                    ac_sqrt1 = {}
                    pe_tr_last = {}
                    dv_fmx = {}
                    pe_ip = {}
                    ac_zx = {}
                    dv_bcdt = {}
                    dv_conv = {}
                    sc_out = {}
                    mview = mv2[:].rearrange("p (r two) -> p two r", two=2)

                    for i in range(NTT):
                        if i >= 2:
                            sc.w("y", s_ac, ac_lnx[i - 2])
                        sc.em("y", lambda y, i=i: y.dma_start(
                            out=u_ring[i % 2][:].rearrange(
                                "p (c d) -> p c d", d=D),
                            in_=u_d[i * TT:(i + 1) * TT, :].rearrange(
                                "(c p) d -> p c d", p=128))
                            .then_inc(d_u.h, 16))
                        d_u.v += 16
                        u_mark[i] = d_u.v

                        sc.w("v", d_u, u_mark[i])
                        if i >= 1:
                            sc.w("v", s_ac, ac_sqrt1[i - 1])

                        def v_stats(v, i=i):
                            ur = u_ring[i % 2]
                            for r in range(4):
                                v.bn_stats(bn12[:, r * 12:r * 12 + 6],
                                           ur[:, r * D:r * D + 512])
                                v.bn_stats(bn12[:, r * 12 + 6:r * 12 + 12],
                                           ur[:, r * D + 512:(r + 1) * D])
                            for r in range(4):
                                _li = v.bn_aggr(mv2[:, 2 * r:2 * r + 2],
                                                bn12[:, r * 12:r * 12 + 12])
                            _li.then_inc(s_dv.h, 1)
                        sc.em("v", v_stats)
                        s_dv.v += 1
                        sc.w("s", s_dv, s_dv.v)
                        sc.em("s", lambda s: s.activation(
                            lnst[:, 8:12], mview[:, 1, :], AF.Sqrt,
                            bias=LN_EPS).then_inc(s_ac.h, 1))
                        s_ac.v += 1
                        ac_sqrt1[i] = s_ac.v
                        sc.w("v", s_ac, s_ac.v)

                        def v_inv(v):
                            v.reciprocal(lnst[:, 8:12], lnst[:, 8:12])\
                                .then_inc(s_dv.h, 1)
                        sc.em("v", v_inv)
                        s_dv.v += 1
                        sc.w("v", s_dv, s_dv.v)
                        sc.em("v", lambda v: v.scalar_tensor_tensor(
                            lnst[:, 12:16], mview[:, 0, :], -1.0,
                            lnst[:, 8:12], ALU.mult, ALU.mult)
                            .then_inc(s_dv.h, 1))
                        s_dv.v += 1
                        sc.w("s", s_dv, s_dv.v)
                        if i >= 2:
                            sc.w("s", s_pe, pe_tr_last.get(i - 2, 0))

                        def s_apply(s, i=i):
                            ur = u_ring[i % 2]
                            lt = lnx_tm[i % 2]
                            for r in range(4):
                                _li = s.activation(lt[:, r * D:(r + 1) * D],
                                             ur[:, r * D:(r + 1) * D],
                                             AF.Identity,
                                             scale=lnst[:, 8 + r:9 + r],
                                             bias=lnst[:, 12 + r:13 + r])
                            _li.then_inc(s_ac.h, 1)
                        sc.em("s", s_apply)
                        s_ac.v += 1
                        ac_lnx[i] = s_ac.v

                        sc.w("t", s_ac, ac_lnx[i])
                        for r in range(4):
                            gr = 4 * i + r
                            if gr >= 2:
                                sc.w("t", s_dv, dv_fmx[gr - 2])

                            def t_tr(t, i=i, r=r):
                                lt = lnx_tm[i % 2]
                                pbb = pk(1024 + (r % 2) * 512).bitcast(BF16)
                                for dd in range(8):
                                    _li = t.transpose(
                                        pbb[:, dd * 128:(dd + 1) * 128],
                                        lt[:, r * D + dd * 128:
                                           r * D + (dd + 1) * 128],
                                        ident_b[:])
                                _li.then_inc(s_pe.h, 1)
                            sc.em("t", t_tr)
                            s_pe.v += 1
                            pe_tr_last[i] = s_pe.v
                            sc.w("v", s_pe, s_pe.v)

                            def v_fmx(v, r=r):
                                pbb = pk(1024 + (r % 2) * 512).bitcast(BF16)
                                dst = lnx_fm[:].rearrange(
                                    "p (d c) -> p d c",
                                    c=512)[:, :, r * 128:(r + 1) * 128]
                                _li = v.tensor_copy(dst, pbb[:].rearrange(
                                    "p (d c) -> p d c", c=128))
                                _li.then_inc(s_dv.h, 1)
                            sc.em("v", v_fmx)
                            s_dv.v += 1
                            dv_fmx[gr] = s_dv.v

                        sc.w("t", s_dv, dv_fmx[4 * i + 3])
                        for ct in range(34):
                            if ct >= 2:
                                if ct - 2 < 32:
                                    sc.w("t", s_ac, ac_zx[(i, ct - 2)])
                                else:
                                    sc.w("t", s_dv, dv_bcdt[(i, ct - 2)])
                            elif i > 0:
                                sc.w("t", s_dv, dv_bcdt[(i - 1, ct + 32)])

                            def t_ip(t, ct=ct):
                                cw_ = 128 if ct < 33 else 32
                                pb = pk((ct % 2) * 512)[0:cw_, :]
                                for dd in range(8):
                                    _li = t.matmul(
                                        pb,
                                        win_bf[:, dd * DP + ct * 128:
                                               dd * DP + ct * 128 + cw_],
                                        lnx_fm[:, dd * 512:(dd + 1) * 512],
                                        start=(dd == 0), stop=(dd == 7))
                                _li.then_inc(s_pe.h, 1)
                            sc.em("t", t_ip)
                            s_pe.v += 1
                            pe_ip[(i, ct)] = s_pe.v

                            if ct < 32:
                                slot = zx_ring[ct % 4]
                                sc.w("s", s_pe, pe_ip[(i, ct)])
                                key = ((i, ct - 4) if ct >= 4
                                       else (i - 1, ct + 28))
                                if key in dv_conv:
                                    sc.w("s", s_dv, dv_conv[key])

                                def s_zx(s, ct=ct, slot=slot):
                                    s.copy(slot[:, 0:3],
                                           hist_b[:, ct * 3:ct * 3 + 3])
                                    s.copy(slot[:, 3:515],
                                           pk((ct % 2) * 512))
                                    _li = s.copy(hist_b[:, ct * 3:ct * 3 + 3],
                                           pk((ct % 2) * 512)[:, 509:512])
                                    _li.then_inc(s_ac.h, 1)
                                sc.em("s", s_zx)
                                s_ac.v += 1
                                ac_zx[(i, ct)] = s_ac.v

                                sc.w("v", s_ac, ac_zx[(i, ct)])
                                if key in sc_out:
                                    sc.w("v", d_sc, sc_out[key])

                                def v_conv(v, ct=ct, slot=slot):
                                    cb = cbuf[ct % 4]
                                    v.tensor_scalar(
                                        cb[:], slot[:, 0:512],
                                        convw_s[:, ct * 4:ct * 4 + 1],
                                        convb_s[:, ct:ct + 1],
                                        ALU.mult, ALU.add)
                                    for k2 in (1, 2, 3):
                                        _li = v.scalar_tensor_tensor(
                                            cb[:], slot[:, k2:512 + k2],
                                            convw_s[:, ct * 4 + k2:
                                                    ct * 4 + k2 + 1],
                                            cb[:], ALU.mult, ALU.add)
                                    _li.then_inc(s_dv.h, 1)
                                sc.em("v", v_conv)
                                s_dv.v += 1
                                dv_conv[(i, ct)] = s_dv.v
                                sc.w("g", s_dv, dv_conv[(i, ct)])
                                dstt = (z_scr[ct] if ct < 16
                                        else x_scr[ct - 16])

                                def g_out(g, i=i, ct=ct, dstt=dstt):
                                    g.dma_start(
                                        out=dstt[:, i * TT:(i + 1) * TT],
                                        in_=cbuf[ct % 4][:])\
                                        .then_inc(d_sc.h, 16)
                                sc.em("g", g_out)
                                d_sc.v += 16
                                sc_out[(i, ct)] = d_sc.v
                            else:
                                sc.w("v", s_pe, pe_ip[(i, ct)])
                                if i > 0:
                                    sc.w("v", d_sc, sc_out[(i - 1, ct)])
                                hoff = (ct - 32) * 3

                                def v_bcdt(v, ct=ct, hoff=hoff):
                                    buf = bcbuf if ct == 32 else dtbuf
                                    rows = 128 if ct == 32 else NH
                                    ob = bcc if ct == 32 else dtc
                                    v.tensor_copy(
                                        buf[0:rows, 0:3],
                                        hist_f[0:rows, hoff:hoff + 3])
                                    v.tensor_copy(
                                        buf[0:rows, 3:515],
                                        pk((ct % 2) * 512)[0:rows, :])
                                    v.tensor_copy(
                                        hist_f[0:rows, hoff:hoff + 3],
                                        pk((ct % 2) * 512)[0:rows, 509:512])
                                    v.tensor_scalar(
                                        ob[0:rows, :], buf[0:rows, 0:512],
                                        convw_s[0:rows,
                                                ct * 4:ct * 4 + 1],
                                        convb_s[0:rows, ct:ct + 1],
                                        ALU.mult, ALU.add)
                                    for k2 in (1, 2, 3):
                                        _li = v.scalar_tensor_tensor(
                                            ob[0:rows, :],
                                            buf[0:rows, k2:512 + k2],
                                            convw_s[0:rows, ct * 4 + k2:
                                                    ct * 4 + k2 + 1],
                                            ob[0:rows, :],
                                            ALU.mult, ALU.add)
                                    _li.then_inc(s_dv.h, 1)
                                sc.em("v", v_bcdt)
                                s_dv.v += 1
                                dv_bcdt[(i, ct)] = s_dv.v
                                sc.w("g", s_dv, dv_bcdt[(i, ct)])
                                dstt = bc_scr if ct == 32 else dt_scr
                                rows = 128 if ct == 32 else NH
                                srcb = bcc if ct == 32 else dtc

                                def g_bc(g, i=i, dstt=dstt, rows=rows,
                                         srcb=srcb):
                                    g.dma_start(
                                        out=dstt[0:rows,
                                                 i * TT:(i + 1) * TT],
                                        in_=srcb[0:rows, :])\
                                        .then_inc(d_sc.h, 16)
                                sc.em("g", g_bc)
                                d_sc.v += 16
                                sc_out[(i, ct)] = d_sc.v
                    barrier(sc)
                    run_phase(sc)

        if upto in ("P1c", "P2", "P3"):
            with ExitStack() as ps:
                wo_bf = sb("wo_bf", [128, 16 * D], BF16, ps)
                z_t = [sb(f"z_t{k}", [128, 16 * TT], BF16, ps)
                       for k in range(2)]
                x_t = [sb(f"x_t{k}", [128, 16 * TT], BF16, ps)
                       for k in range(2)]
                bcg = [sb(f"bcg{k}", [128, TT], F32, ps) for k in range(2)]
                dtg = [sb(f"dtg{k}", [NH, TT], F32, ps) for k in range(2)]
                u2 = [sb(f"u2_{k}", [128, D], F32, ps) for k in range(2)]
                clo = sb("clo", [64, TT], F32, ps)
                Bbf = [sb(f"Bbf{k}", [64, TT], BF16, ps) for k in range(2)]
                Cbf = [sb(f"Cbf{k}", [64, TT], BF16, ps) for k in range(2)]
                dts = sb("dts", [NH, TT], F32, ps)
                af = sb("af", [NH, TT], F32, ps)
                sl4 = sb("sl4", [NH, 8], F32, ps)
                eslr = sb("eslr", [1, 128], F32, ps)
                ebc4 = [sb(f"ebc4_{k}", [64, 128], F32, ps)
                        for k in range(2)]
                a_tm4 = sb("a_tm4", [128, 128], F32, ps)
                dt_tm4 = sb("dt_tm4", [128, 128], F32, ps)
                s_tm4 = sb("s_tm4", [128, 128], F32, ps)
                es4 = [sb(f"es4_{k}", [128, 128], F32, ps)
                       for k in range(2)]
                emns4 = sb("emns4", [128, 128], F32, ps)
                scol4 = [sb(f"scol4_{k}", [128, 128], F32, ps)
                         for k in range(2)]
                B_tm4 = [sb(f"B_tm4_{k}", [128, 256], BF16, ps)
                         for k in range(2)]
                MT4b = [sb(f"MT4b_{k}", [128, 512], BF16, ps)
                        for k in range(2)]
                xmod = [sb(f"xmod{k}", [128, 2048], BF16, ps)
                        for k in range(2)]
                xD = [sb(f"xD{k}", [128, 2048], BF16, ps) for k in range(2)]
                y2b = sb("y2b", [128, 2048], F32, ps)
                y3b = sb("y3b", [128, 2048], BF16, ps)
                yg = sb("yg", [128, 2048], BF16, ps)
                ybn = sb("ybn", [128, 24], F32, ps)
                yms = sb("yms", [128, 4], F32, ps)
                xm = [sb(f"xm_{k}", [128, D], F32, ps) for k in range(2)]
                bnx = sb("bnx", [128, 12], F32, ps)
                mvx = sb("mvx", [128, 2], F32, ps)
                v2c = sb("v2c", [128, 8], F32, ps)
                iv4 = sb("iv4", [128, 4], F32, ps)
                stf = [state0, state1]
                stb = [state0b, state1b]

                sc = Sched()
                dv_m = {}
                pe_m = {}
                ac_m = {}
                gp_m = {}
                ld_tile = {}
                ld_clo = {}
                ld_u2 = {}
                wr_xm = {}

                def FEN():
                    sc.w("v", s_dv, s_dv.v)

                wo_done = {}
                for k in range(16):
                    if k >= 2:
                        sc.w("y", s_dv, wo_done[k - 2])
                    sc.em("y", lambda y, k=k: y.dma_start(
                        out=u2[k % 2][:],
                        in_=woT_d[k * 128:(k + 1) * 128, :])
                        .then_inc(d_w.h, 16))
                    d_w.v += 16
                    sc.w("v", d_w, d_w.v)
                    sc.em("v", lambda v, k=k: v.tensor_scalar(
                        wo_bf[:, k * D:(k + 1) * D], u2[k % 2][:],
                        nwcol_s[:, k:k + 1], None, ALU.mult)
                        .then_inc(s_dv.h, 1))
                    s_dv.v += 1
                    wo_done[k] = s_dv.v

                def emit_loads(j):
                    if j >= 2:
                        sc.w("y", s_dv, dv_m[("gate", 4 * (j - 2) + 3)])
                        sc.w("y", s_pe, pe_m[("xt", 4 * (j - 2) + 3, 1)])
                    for part in range(4):
                        for scr_t, dst_t in ((z_scr, z_t), (x_scr, x_t)):
                            sc.em("y", lambda y, j=j, part=part, scr_t=scr_t,
                                  dst_t=dst_t: y.dma_start(
                                      out=dst_t[j % 2][:].rearrange(
                                          "p (c t) -> p c t",
                                          t=TT)[:, part * 4:(part + 1) * 4],
                                      in_=scr_t[part * 4:(part + 1) * 4, :,
                                                j * TT:(j + 1) * TT]
                                      .transpose([1, 0, 2]))
                                  .then_inc(d_u.h, 16))
                            d_u.v += 16
                    if j >= 2:
                        sc.w("y", s_dv, dv_m[("prep", j - 2)])
                    sc.em("y", lambda y, j=j: y.dma_start(
                        out=bcg[j % 2][:],
                        in_=bc_scr[:, j * TT:(j + 1) * TT])
                        .then_inc(d_u.h, 16))
                    d_u.v += 16
                    sc.em("y", lambda y, j=j: y.dma_start(
                        out=dtg[j % 2][:],
                        in_=dt_scr[:, j * TT:(j + 1) * TT])
                        .then_inc(d_u.h, 16))
                    d_u.v += 16
                    ld_tile[j] = d_u.v

                def emit_clo(j):
                    sc.w("g", d_u, ld_tile[j])
                    if j >= 1:
                        sc.w("g", s_dv, dv_m[("prep", j - 1)])
                    sc.em("g", lambda g, j=j: g.dma_start(
                        out=clo[:], in_=bcg[j % 2][64:128, :])
                        .then_inc(d_mv.h, 16))
                    d_mv.v += 16
                    ld_clo[j] = d_mv.v

                def emit_dts(j):
                    sc.w("s", d_u, ld_tile[j])

                    def f(s, j=j):
                        s.activation(dts[:], dtg[j % 2][:], AF.Exp,
                                     bias=dtb_s[:])
                        _li = s.activation(dts[:], dts[:], AF.Ln, bias=1.0)
                        _li.then_inc(s_ac.h, 1)
                    sc.em("s", f)
                    s_ac.v += 1
                    ac_m[("dts", j)] = s_ac.v

                def emit_prep(j):
                    sc.w("v", s_ac, ac_m[("dts", j)])
                    sc.w("v", d_mv, ld_clo[j])

                    def f(v, j=j):
                        v.tensor_scalar(af[:], dts[:], alog_s[:], None,
                                        ALU.mult)
                        v.tensor_copy(Bbf[j % 2][:], bcg[j % 2][0:64, :])
                        v.tensor_copy(Cbf[j % 2][:], clo[:])
                        for q4 in range(4):
                            _li = v.reduce_sum(
                                sl4[:, q4:q4 + 1],
                                af[:, q4 * 128:(q4 + 1) * 128], AXX)
                        _li.then_inc(s_dv.h, 1)
                    sc.em("v", f)
                    s_dv.v += 1
                    dv_m[("prep", j)] = s_dv.v

                def emit_smalls_tr(j):
                    sc.w("t", s_dv, dv_m[("prep", j)])
                    if j >= 1:
                        sc.w("t", s_dv, dv_m[("ebc4", j - 1)])

                    def f(t, j=j):
                        for r4 in range(4):
                            t.transpose(psum[:, 2560 + r4 * 32:
                                             2592 + r4 * 32],
                                        af[0:NH, r4 * 128:(r4 + 1) * 128],
                                        identf[0:NH, 0:NH])
                            t.transpose(psum[:, 2688 + r4 * 32:
                                             2720 + r4 * 32],
                                        dts[0:NH, r4 * 128:(r4 + 1) * 128],
                                        identf[0:NH, 0:NH])
                            _li = t.transpose(
                                psum[:, 2816 + r4 * 32:2848 + r4 * 32]
                                .bitcast(BF16)[:, 0:64],
                                Bbf[j % 2][:, r4 * 128:(r4 + 1) * 128],
                                ident_b[0:64, 0:64])
                        _li.then_inc(s_pe.h, 1)
                    sc.em("t", f)
                    s_pe.v += 1
                    pe_m[("smtr", j)] = s_pe.v

                def emit_smalls_evac(j):
                    sc.w("v", s_pe, pe_m[("smtr", j)])

                    def f(v, j=j):
                        v.tensor_copy(a_tm4[:], psum[:, 2560:2688])
                        v.tensor_copy(dt_tm4[:], psum[:, 2688:2816])
                        _li = v.tensor_copy(
                            B_tm4[j % 2][:], psum[:, 2816:2944].bitcast(BF16))
                        _li.then_inc(s_dv.h, 1)
                    sc.em("v", f)
                    s_dv.v += 1
                    dv_m[("smev", j)] = s_dv.v

                def emit_sMM(j):
                    sc.w("t", s_dv, dv_m[("smev", j)])
                    sc.em("t", lambda t: t.matmul(
                        psum[:, 2944:3072], tril_f[:], a_tm4[:],
                        start=True, stop=True).then_inc(s_pe.h, 1))
                    s_pe.v += 1
                    pe_m[("sMM", j)] = s_pe.v

                def emit_s4evac(j):
                    sc.w("v", s_pe, pe_m[("sMM", j)])
                    sc.em("v", lambda v: v.tensor_copy(
                        s_tm4[:], psum[:, 2944:3072]).then_inc(s_dv.h, 1))
                    s_dv.v += 1
                    dv_m[("s4ev", j)] = s_dv.v

                def emit_es4(j):
                    sc.w("s", s_dv, dv_m[("s4ev", j)])

                    def f(s, j=j):
                        s.activation(es4[j % 2][:], s_tm4[:], AF.Exp)
                        _li = s.activation(emns4[:], s_tm4[:], AF.Exp,
                                           scale=-1.0)
                        _li.then_inc(s_ac.h, 1)
                    sc.em("s", f)
                    s_ac.v += 1
                    ac_m[("es4", j)] = s_ac.v

                def emit_expsl(j):
                    sc.w("s", s_dv, dv_m[("prep", j)])
                    sc.em("s", lambda s: s.activation(
                        sl4[:, 4:8], sl4[:, 0:4], AF.Exp)
                        .then_inc(s_ac.h, 1))
                    s_ac.v += 1
                    ac_m[("expsl", j)] = s_ac.v

                def emit_scol4(j):
                    sc.w("v", s_ac, ac_m[("es4", j)])
                    sc.em("v", lambda v, j=j: v.tensor_tensor(
                        scol4[j % 2][:], dt_tm4[:], emns4[:], ALU.mult)
                        .then_inc(s_dv.h, 1))
                    s_dv.v += 1
                    dv_m[("scol", j)] = s_dv.v

                def emit_cbt4(j):
                    sc.w("t", s_dv, dv_m[("prep", j)])
                    if j >= 1:
                        sc.w("t", s_dv, dv_m[("mt4", j - 1)])

                    def f(t, j=j):
                        for r4 in range(4):
                            _li = t.matmul(
                                psum[:, 2048 + r4 * 128:2176 + r4 * 128],
                                Bbf[j % 2][:, r4 * 128:(r4 + 1) * 128],
                                Cbf[j % 2][:, r4 * 128:(r4 + 1) * 128],
                                start=True, stop=True)
                        _li.then_inc(s_pe.h, 1)
                    sc.em("t", f)
                    s_pe.v += 1
                    pe_m[("cbt4", j)] = s_pe.v

                def emit_mt4(j):
                    sc.w("v", s_pe, pe_m[("cbt4", j)])
                    sc.em("v", lambda v, j=j: v.tensor_tensor(
                        MT4b[j % 2][:].rearrange("p (c t) -> p c t", t=128),
                        psum[:, 2048:2560].rearrange("p (c t) -> p c t",
                                                     t=128),
                        tril_f[:].unsqueeze(1).broadcast_to((128, 4, 128)),
                        ALU.mult).then_inc(s_dv.h, 1))
                    s_dv.v += 1
                    dv_m[("mt4", j)] = s_dv.v

                def emit_esltr(j):
                    sc.w("t", s_ac, ac_m[("expsl", j)])
                    sc.w("t", s_dv, dv_m[("smev", j)])

                    def f(t):
                        for r4 in range(4):
                            _li = t.transpose(
                                psum[0:1, 2560 + r4 * 32:2592 + r4 * 32],
                                sl4[0:NH, 4 + r4:5 + r4],
                                identf[0:NH, 0:NH])
                        _li.then_inc(s_pe.h, 1)
                    sc.em("t", f)
                    s_pe.v += 1
                    pe_m[("esltr", j)] = s_pe.v

                def emit_eslr(j):
                    sc.w("v", s_pe, pe_m[("esltr", j)])
                    sc.em("v", lambda v: v.tensor_copy(
                        eslr[:], psum[0:1, 2560:2688]).then_inc(s_dv.h, 1))
                    s_dv.v += 1
                    dv_m[("eslr", j)] = s_dv.v

                def emit_ebcmm(j):
                    sc.w("t", s_dv, dv_m[("eslr", j)])
                    sc.em("t", lambda t: t.matmul(
                        psum[0:64, 2688:2816], ones_f[0:1, 0:64], eslr[:],
                        start=True, stop=True).then_inc(s_pe.h, 1))
                    s_pe.v += 1
                    pe_m[("ebcmm", j)] = s_pe.v

                def emit_ebc4(j):
                    sc.w("v", s_pe, pe_m[("ebcmm", j)])
                    sc.em("v", lambda v, j=j: v.tensor_copy(
                        ebc4[j % 2][:], psum[0:64, 2688:2816])
                        .then_inc(s_dv.h, 1))
                    s_dv.v += 1
                    dv_m[("ebc4", j)] = s_dv.v

                def emit_silu(j):
                    sc.w("s", d_u, ld_tile[j])
                    sc.em("s", lambda s, j=j: s.activation(
                        z_t[j % 2][:], z_t[j % 2][:], AF.Silu)
                        .then_inc(s_ac.h, 1))
                    s_ac.v += 1
                    ac_m[("silu", j)] = s_ac.v

                def emit_u2(cc):
                    if cc < 2:
                        sc.w("y", s_dv, wo_done[14 + cc])
                    else:
                        sc.w("y", s_dv, dv_m[("xm", cc - 2)])
                    sc.em("y", lambda y, cc=cc: y.dma_start(
                        out=u2[cc % 2][:],
                        in_=u_d[cc * 128:(cc + 1) * 128, :])
                        .then_inc(d_u.h, 16))
                    d_u.v += 16
                    ld_u2[cc] = d_u.v

                def emit_xt(cc, hg):
                    i2, r4 = (cc // 4) % 2, cc % 4
                    if hg == 0:
                        if cc >= 2:
                            sc.w("t", s_dv, dv_m[("state", cc - 2, 1)])
                        if cc >= 1:
                            sc.w("t", s_dv, dv_m[("xmod", cc - 1, 1)])
                        sc.w("t", d_u, ld_tile[cc // 4])

                    def f(t, i2=i2, r4=r4, hg=hg):
                        pbb = psum[:, 3072 + hg * 512:3584 + hg * 512]\
                            .bitcast(BF16)
                        for k in range(8):
                            cx = hg * 8 + k
                            _li = t.transpose(
                                pbb[:, k * 128:(k + 1) * 128],
                                x_t[i2][:, cx * TT + r4 * 128:
                                        cx * TT + (r4 + 1) * 128],
                                ident_b[:])
                        _li.then_inc(s_pe.h, 1)
                    sc.em("t", f)
                    s_pe.v += 1
                    pe_m[("xt", cc, hg)] = s_pe.v

                def emit_xmod(cc, hg):
                    i2, r4 = (cc // 4) % 2, cc % 4
                    sc.w("v", s_pe, pe_m[("xt", cc, hg)])

                    def f(v, cc=cc, i2=i2, r4=r4, hg=hg):
                        pbb = psum[:, 3072 + hg * 512:3584 + hg * 512]\
                            .bitcast(BF16)
                        src3 = pbb[:].rearrange("p (h e) -> p h e", e=64)
                        sc3 = scol4[i2][:, r4 * 32 + hg * 16:
                                        r4 * 32 + (hg + 1) * 16]\
                            .unsqueeze(2).broadcast_to((128, 16, 64))
                        dst3 = xmod[cc % 2][:, hg * 1024:(hg + 1) * 1024]\
                            .rearrange("p (h e) -> p h e", e=64)
                        db3 = dbc_b[:, hg * 1024:(hg + 1) * 1024]\
                            .rearrange("p (h e) -> p h e", e=64)
                        dstD = xD[cc % 2][:, hg * 1024:(hg + 1) * 1024]\
                            .rearrange("p (h e) -> p h e", e=64)
                        v.tensor_tensor(dst3, src3, sc3, ALU.mult)
                        _li = v.tensor_tensor(dstD, src3, db3, ALU.mult)
                        _li.then_inc(s_dv.h, 1)
                    sc.em("v", f)
                    s_dv.v += 1
                    dv_m[("xmod", cc, hg)] = s_dv.v

                def emit_y(cc, hg):
                    i2, r4 = (cc // 4) % 2, cc % 4
                    base = hg * 1024
                    if hg == 0:
                        if cc >= 2:
                            sc.w("t", s_dv, dv_m[("xm", cc - 2)])
                        sc.w("t", s_dv, dv_m[("xmod", cc, 1)])
                    else:
                        if cc >= 1:
                            sc.w("t", s_dv, dv_m[("gate", cc - 1)])
                    if cc >= 1:
                        sc.w("t", s_gp, gp_m[("sttb", cc - 1, hg)])

                    def f(t, cc=cc, i2=i2, r4=r4, base=base, hg=hg):
                        mt = MT4b[i2][:, r4 * 128:(r4 + 1) * 128]
                        cb = Cbf[i2][:, r4 * 128:(r4 + 1) * 128]
                        for h2 in range(2):
                            t.matmul(pk(base + h2 * 512), mt,
                                     xmod[cc % 2][:, base + h2 * 512:
                                                  base + (h2 + 1) * 512],
                                     start=True, stop=False)
                        for h2 in range(2):
                            _li = t.matmul(
                                pk(base + h2 * 512), cb,
                                stb[hg][:, h2 * 512:(h2 + 1) * 512],
                                start=False, stop=True)
                        _li.then_inc(s_pe.h, 1)
                    sc.em("t", f)
                    s_pe.v += 1
                    pe_m[("y", cc, hg)] = s_pe.v

                def emit_y1(cc, hg):
                    i2, r4 = (cc // 4) % 2, cc % 4
                    base = hg * 1024
                    sc.w("v", s_pe, pe_m[("y", cc, hg)])

                    def f(v, i2=i2, r4=r4, base=base, hg=hg):
                        p3 = pk(base, 1024)[:].rearrange(
                            "p (h e) -> p h e", e=64)
                        e3 = es4[i2][:, r4 * 32 + hg * 16:
                                     r4 * 32 + (hg + 1) * 16]\
                            .unsqueeze(2).broadcast_to((128, 16, 64))
                        d3 = y2b[:, base:base + 1024].rearrange(
                            "p (h e) -> p h e", e=64)
                        _li = v.tensor_tensor(d3, p3, e3, ALU.mult)
                        _li.then_inc(s_dv.h, 1)
                    sc.em("v", f)
                    s_dv.v += 1
                    dv_m[("y1", cc, hg)] = s_dv.v

                def emit_dh(cc, hg):
                    i2, r4 = (cc // 4) % 2, cc % 4
                    if hg == 0:
                        sc.w("t", s_dv, dv_m[("y1", cc, 1)])
                        base_dh = 1024
                    else:
                        if cc + 1 <= 31:
                            sc.w("t", s_dv, dv_m[("xmod", cc + 1, 1)])
                        sc.w("t", s_dv, dv_m[("state", cc, 0)])
                        base_dh = 3072

                    def f(t, cc=cc, i2=i2, r4=r4, hg=hg, base_dh=base_dh):
                        for h2 in range(2):
                            _li = t.matmul(
                                psum[0:64, base_dh + h2 * 512:
                                     base_dh + (h2 + 1) * 512],
                                B_tm4[i2][:, r4 * 64:(r4 + 1) * 64],
                                xmod[cc % 2][:, hg * 1024 + h2 * 512:
                                             hg * 1024 + (h2 + 1) * 512],
                                start=True, stop=True)
                        _li.then_inc(s_pe.h, 1)
                    sc.em("t", f)
                    s_pe.v += 1
                    pe_m[("dh", cc, hg)] = s_pe.v

                def emit_state(cc, hg):
                    i2, r4 = (cc // 4) % 2, cc % 4
                    sc.w("v", s_pe, pe_m[("dh", cc, hg)])

                    def f(v, i2=i2, r4=r4, hg=hg):
                        dhp = psum[0:64, 1024:2048] if hg == 0 else \
                            psum[0:64, 3072:4096]
                        v.tensor_tensor(stf[hg][:], stf[hg][:], dhp,
                                        ALU.add)
                        e3 = ebc4[i2][:, r4 * 32 + hg * 16:
                                      r4 * 32 + (hg + 1) * 16]\
                            .unsqueeze(2).broadcast_to((64, 16, 64))
                        s3 = stf[hg][:].rearrange("p (h e) -> p h e", e=64)
                        o3 = stf[hg][:].rearrange("p (h e) -> p h e", e=64)
                        _li = v.tensor_tensor(o3, s3, e3, ALU.mult)
                        _li.then_inc(s_dv.h, 1)
                    sc.em("v", f)
                    s_dv.v += 1
                    dv_m[("state", cc, hg)] = s_dv.v

                def emit_sttb(cc, hg):
                    sc.w("g", s_dv, dv_m[("state", cc, hg)])
                    sc.em("g", lambda g, hg=hg: g.tensor_copy(
                        stb[hg][:], stf[hg][:]).then_inc(s_gp.h, 1))
                    s_gp.v += 1
                    gp_m[("sttb", cc, hg)] = s_gp.v

                def emit_y2(cc, hg):
                    if hg == 0 and cc >= 1:
                        sc.w("v", s_ac, ac_m[("y3", cc - 1)])

                    def f(v, cc=cc, hg=hg):
                        b = hg * 1024
                        v.tensor_tensor(y2b[:, b:b + 1024],
                                        y2b[:, b:b + 1024],
                                        xD[cc % 2][:, b:b + 1024], ALU.add)
                        v.bn_stats(ybn[:, hg * 12:hg * 12 + 6],
                                   y2b[:, b:b + 512])
                        _li = v.bn_stats(ybn[:, hg * 12 + 6:hg * 12 + 12],
                                         y2b[:, b + 512:b + 1024])
                        _li.then_inc(s_dv.h, 1)
                    sc.em("v", f)
                    s_dv.v += 1
                    dv_m[("y2", cc, hg)] = s_dv.v

                def emit_aggr(cc):
                    FEN()
                    sc.em("v", lambda v: v.bn_aggr(yms[:, 0:2], ybn[:])
                          .then_inc(s_dv.h, 1))
                    s_dv.v += 1

                def emit_stt(cc):
                    FEN()
                    sc.em("v", lambda v: v.scalar_tensor_tensor(
                        yms[:, 2:3], yms[:, 0:1], yms[:, 0:1], yms[:, 1:2],
                        ALU.mult, ALU.add).then_inc(s_dv.h, 1))
                    s_dv.v += 1
                    dv_m[("stt", cc)] = s_dv.v

                def emit_sqrt(cc):
                    sc.w("s", s_dv, dv_m[("stt", cc)])
                    sc.em("s", lambda s: s.activation(
                        yms[:, 3:4], yms[:, 2:3], AF.Sqrt, bias=FP32_EPS)
                        .then_inc(s_ac.h, 1))
                    s_ac.v += 1
                    ac_m[("sqrt", cc)] = s_ac.v

                def emit_recip(cc):
                    sc.w("v", s_ac, ac_m[("sqrt", cc)])
                    sc.em("v", lambda v: v.reciprocal(
                        yms[:, 3:4], yms[:, 3:4]).then_inc(s_dv.h, 1))
                    s_dv.v += 1
                    dv_m[("recip", cc)] = s_dv.v

                def emit_acty3(cc):
                    sc.w("s", s_dv, dv_m[("recip", cc)])
                    sc.em("s", lambda s: s.activation(
                        y3b[:], y2b[:], AF.Copy, scale=yms[:, 3:4])
                        .then_inc(s_ac.h, 1))
                    s_ac.v += 1
                    ac_m[("y3", cc)] = s_ac.v

                def emit_y3T(cc):
                    sc.w("t", s_ac, ac_m[("y3", cc)])
                    sc.w("t", s_dv, dv_m[("state", cc, 0)])

                    def f(t):
                        pbb = pk(1024, 1024).bitcast(BF16)
                        for k in range(16):
                            _li = t.transpose(
                                pbb[:, k * 128:(k + 1) * 128],
                                y3b[:, k * 128:(k + 1) * 128], ident_b[:])
                        _li.then_inc(s_pe.h, 1)
                    sc.em("t", f)
                    s_pe.v += 1
                    pe_m[("y3T", cc)] = s_pe.v

                def emit_gate(cc):
                    i2, r4 = (cc // 4) % 2, cc % 4
                    sc.w("v", s_pe, pe_m[("y3T", cc)])
                    if cc >= 1:
                        sc.w("v", s_pe, pe_m[("op", cc - 1)])
                    sc.w("v", s_ac, ac_m[("silu", cc // 4)])

                    def f(v, i2=i2, r4=r4):
                        pbb = pk(1024, 1024).bitcast(BF16)
                        zsl = z_t[i2][:].rearrange(
                            "p (c t) -> p c t",
                            t=TT)[:, :, r4 * 128:(r4 + 1) * 128]
                        _li = v.tensor_tensor(
                            yg[:].rearrange("p (c t) -> p c t", t=128),
                            pbb[:].rearrange("p (c t) -> p c t", t=128),
                            zsl, ALU.mult)
                        _li.then_inc(s_dv.h, 1)
                    sc.em("v", f)
                    s_dv.v += 1
                    dv_m[("gate", cc)] = s_dv.v

                def emit_op(cc, half):
                    if half == 0:
                        sc.w("t", s_dv, dv_m[("gate", cc)])
                        if cc + 1 <= 31:
                            sc.w("t", s_dv, dv_m[("y1", cc + 1, 0)])
                        else:
                            sc.w("t", s_dv, dv_m[("xm", cc - 1)])

                    def f(t, cc=cc, half=half):
                        for k in range(half * 8, half * 8 + 8):
                            for oh in range(2):
                                _li = t.matmul(
                                    pk(oh * 512),
                                    yg[:, k * 128:(k + 1) * 128],
                                    wo_bf[:, k * D + oh * 512:
                                          k * D + (oh + 1) * 512],
                                    start=(k == 0), stop=(k == 15))
                        if half == 1:
                            _li.then_inc(s_pe.h, 1)
                    sc.em("t", f)
                    if half == 1:
                        s_pe.v += 1
                        pe_m[("op", cc)] = s_pe.v

                def emit_xm(cc):
                    sc.w("v", s_pe, pe_m[("op", cc)])
                    sc.w("v", d_u, ld_u2[cc])
                    if cc >= 2:
                        sc.w("v", d_sc, wr_xm[cc - 2])

                    def f(v, cc=cc):
                        xmc = xm[cc % 2]
                        for oh in range(2):
                            v.tensor_tensor(
                                xmc[:, oh * 512:(oh + 1) * 512],
                                pk(oh * 512),
                                u2[cc % 2][:, oh * 512:(oh + 1) * 512],
                                ALU.add)
                        v.bn_stats(bnx[:, 0:6], xmc[:, 0:512])
                        v.bn_stats(bnx[:, 6:12], xmc[:, 512:1024])
                        v.reduce_max(mmt1[:, cc:cc + 1], xmc[:], AXX)
                        v.tensor_reduce(mmt1[:, 32 + cc:33 + cc], xmc[:],
                                        AXX, ALU.min)
                        _li = v.bn_aggr(mvx[:], bnx[:])
                        _li.then_inc(s_dv.h, 1)
                    sc.em("v", f)
                    s_dv.v += 1
                    dv_m[("xm", cc)] = s_dv.v

                def emit_xm2(cc):
                    if cc + 1 > 31:
                        FEN()

                    def f(v, cc=cc):
                        i2 = (cc // 4) % 2
                        col = i2 * 4 + cc % 4
                        v.tensor_copy(m2_s[:, cc:cc + 1], mvx[:, 0:1])
                        _li = v.tensor_scalar(
                            v2c[:, col:col + 1], mvx[:, 1:2], LN_EPS, None,
                            ALU.add)
                        _li.then_inc(s_dv.h, 1)
                    sc.em("v", f)
                    s_dv.v += 1
                    dv_m[("xm2", cc)] = s_dv.v

                def emit_dma(cc):
                    sc.w("g", s_dv, dv_m[("xm", cc)])
                    sc.em("g", lambda g, cc=cc: g.dma_start(
                        out=xmid_scr[cc * 128:(cc + 1) * 128, :],
                        in_=xm[cc % 2][:]).then_inc(d_sc.h, 16))
                    d_sc.v += 16
                    wr_xm[cc] = d_sc.v

                def emit_iv_sqrt(tt):
                    sc.w("s", s_dv, dv_m[("xm2", 4 * tt + 3)])
                    if tt >= 1:
                        sc.w("s", s_dv, dv_m[("ivr", tt - 1)])
                    sc.em("s", lambda s, tt=tt: s.activation(
                        iv4[:], v2c[:, (tt % 2) * 4:(tt % 2) * 4 + 4],
                        AF.Sqrt).then_inc(s_ac.h, 1))
                    s_ac.v += 1
                    ac_m[("ivs", tt)] = s_ac.v

                def emit_iv_rec(tt):
                    sc.w("v", s_ac, ac_m[("ivs", tt)])
                    sc.em("v", lambda v, tt=tt: v.reciprocal(
                        inv2_s[:, 4 * tt:4 * tt + 4], iv4[:])
                        .then_inc(s_dv.h, 1))
                    s_dv.v += 1
                    dv_m[("ivr", tt)] = s_dv.v

                emit_loads(0)
                emit_clo(0)
                emit_dts(0)
                emit_prep(0)
                emit_smalls_tr(0)
                emit_smalls_evac(0)
                emit_sMM(0)
                emit_s4evac(0)
                emit_es4(0)
                emit_expsl(0)
                emit_scol4(0)
                emit_cbt4(0)
                emit_esltr(0)
                emit_mt4(0)
                emit_eslr(0)
                emit_ebcmm(0)
                emit_ebc4(0)
                emit_silu(0)
                emit_xt(0, 0)
                emit_xt(0, 1)
                emit_xmod(0, 0)
                emit_xmod(0, 1)

                for c in range(36):
                    i, r = c // 4, c % 4
                    j = i + 1
                    if 1 <= c <= 32:
                        emit_y3T(c - 1)
                    if 2 <= c <= 33:
                        emit_xm(c - 2)
                    if 1 <= c <= 32:
                        emit_gate(c - 1)
                    if 2 <= c <= 33:
                        emit_xm2(c - 2)
                        emit_dma(c - 2)
                    if r == 0 and 1 <= i <= 7:
                        emit_ebc4(i)
                    if r == 0 and 1 <= j <= 7:
                        emit_loads(j)
                        emit_clo(j)
                    if c <= 31:
                        emit_u2(c)
                    if c + 1 <= 31:
                        emit_xt(c + 1, 0)
                        emit_xt(c + 1, 1)
                    if r == 1 and j <= 7:
                        emit_dts(j)
                        emit_prep(j)
                    if r == 2 and j <= 7:
                        emit_smalls_tr(j)
                    if c <= 31:
                        emit_y(c, 0)
                        emit_y(c, 1)
                        emit_y1(c, 0)
                    if c + 1 <= 31:
                        emit_xmod(c + 1, 0)
                    if r == 2 and j <= 7:
                        emit_smalls_evac(j)
                        emit_sMM(j)
                    if c <= 31:
                        emit_y1(c, 1)
                        emit_dh(c, 0)
                        emit_state(c, 0)
                    if r == 2 and j <= 7:
                        emit_s4evac(j)
                        emit_es4(j)
                        emit_expsl(j)
                    if r == 2 and 1 <= i <= 8:
                        emit_iv_sqrt(i - 1)
                    if 1 <= c <= 32:
                        emit_op(c - 1, 0)
                    if c <= 31:
                        emit_y2(c, 0)
                    if c + 1 <= 31:
                        emit_xmod(c + 1, 1)
                    if c <= 31:
                        emit_y2(c, 1)
                    if r == 2 and j <= 7:
                        emit_scol4(j)
                    if r == 2 and 1 <= i <= 8:
                        emit_iv_rec(i - 1)
                    if c <= 31:
                        emit_aggr(c)
                        emit_stt(c)
                        emit_sqrt(c)
                        emit_dh(c, 1)
                    if 1 <= c <= 32:
                        emit_op(c - 1, 1)
                    if c <= 31:
                        emit_recip(c)
                        emit_state(c, 1)
                        emit_acty3(c)
                        emit_sttb(c, 0)
                        emit_sttb(c, 1)
                    if r == 1 and j <= 7:
                        emit_silu(j)
                    if r == 3 and j <= 7:
                        emit_cbt4(j)
                        emit_esltr(j)
                        emit_mt4(j)
                        emit_eslr(j)
                        emit_ebcmm(j)

                sc.w("v", s_dv, dv_m[("ivr", 7)])

                def v_mm1a(v):
                    v.tensor_tensor(mmt1[:, 0:32], mmt1[:, 0:32], m2_s[:],
                                    ALU.subtract)
                    v.tensor_tensor(mmt1[:, 32:64], mmt1[:, 32:64], m2_s[:],
                                    ALU.subtract).then_inc(s_dv.h, 1)
                sc.em("v", v_mm1a)
                s_dv.v += 1
                sc.w("v", s_dv, s_dv.v)

                def v_mm1b(v):
                    v.tensor_tensor(mmt1[:, 0:32], mmt1[:, 0:32], inv2_s[:],
                                    ALU.mult)
                    v.tensor_tensor(mmt1[:, 32:64], mmt1[:, 32:64],
                                    inv2_s[:], ALU.mult).then_inc(s_dv.h, 1)
                sc.em("v", v_mm1b)
                s_dv.v += 1
                sc.w("v", s_dv, s_dv.v)

                def v_mm1c(v):
                    v.reduce_max(mmt1[:, 0:1], mmt1[:, 0:32], AXX)
                    v.tensor_reduce(mmt1[:, 1:2], mmt1[:, 32:64], AXX,
                                    ALU.min).then_inc(s_dv.h, 1)
                sc.em("v", v_mm1c)
                s_dv.v += 1
                sc.w("v", s_dv, s_dv.v)
                sc.em("v", lambda v: v.tensor_scalar(
                    mmt1[:, 1:2], mmt1[:, 1:2], -1.0, None, ALU.mult)
                    .then_inc(s_dv.h, 1))
                s_dv.v += 1
                sc.w("t", s_dv, s_dv.v)

                def t_trm1(t):
                    t.transpose(psum[0:1, 0:128], mmt1[:, 0:1], identf[:])
                    _li = t.transpose(psum[0:1, 128:256], mmt1[:, 1:2],
                                      identf[:])
                    _li.then_inc(s_pe.h, 1)
                sc.em("t", t_trm1)
                s_pe.v += 1
                sc.w("v", s_pe, s_pe.v)

                def v_mm2(v):
                    v.reduce_max(mmrow[:, 0:1], psum[0:1, 0:128], AXX)
                    _li = v.reduce_max(mmrow[:, 1:2], psum[0:1, 128:256],
                                       AXX)
                    _li.then_inc(s_dv.h, 1)
                sc.em("v", v_mm2)
                s_dv.v += 1
                barrier(sc)
                run_phase(sc)
            ms.close()

        def collective(sc, in_t, out_t, after_fn):
            sc.em("g", lambda g: g.dma_start(out=in_t.ap(), in_=mmrow[:])
                  .then_inc(d_cc.h, 16))
            d_cc.v += 16
            sc.w("g", d_cc, d_cc.v)
            sc.em("g", lambda g: g.collective_compute(
                "AllReduce", ALU.max,
                replica_groups=[list(range(len(in_maps)))],
                ins=[in_t.ap().opt()],
                outs=[out_t.ap().opt()]).then_inc(s_gp.h, 1))
            s_gp.v += 1
            sc.w("g", s_gp, s_gp.v)
            sc.em("g", lambda g: g.dma_start(out=ccr[:], in_=out_t.ap())
                  .then_inc(d_cc.h, 16))
            d_cc.v += 16
            sc.w("v", d_cc, d_cc.v)
            after_fn(sc)

        if upto in ("P2", "P3"):
            sc = Sched()

            def after1(sc):
                def v_f1(v):
                    v.tensor_tensor(sc_s[:, 8:9], ccr[:, 0:1], ccr[:, 1:2],
                                    ALU.add).then_inc(s_dv.h, 1)
                sc.em("v", v_f1)
                s_dv.v += 1
                sc.w("v", s_dv, s_dv.v)
                sc.em("v", lambda v: v.tensor_scalar(
                    sc_s[:, 9:10], sc_s[:, 8:9], 1.0 / 255.0, 1e-8,
                    ALU.mult, ALU.max).then_inc(s_dv.h, 1))
                s_dv.v += 1
                sc.w("v", s_dv, s_dv.v)
                sc.em("v", lambda v: v.reciprocal(
                    sc_s[:, 10:11], sc_s[:, 9:10]).then_inc(s_dv.h, 1))
                s_dv.v += 1
                sc.w("v", s_dv, s_dv.v)
                sc.em("v", lambda v: v.tensor_tensor(
                    sc_s[:, 11:12], ccr[:, 1:2], sc_s[:, 10:11],
                    ALU.mult).then_inc(s_dv.h, 1))
                s_dv.v += 1
                iscr = sb("cc_i32", [1, 2], I32)
                sc.w("v", s_dv, s_dv.v)

                sc.em("v", lambda v: v.tensor_copy(
                    iscr[:, 0:1], sc_s[:, 11:12]).then_inc(s_dv.h, 1))
                s_dv.v += 1
                sc.w("v", s_dv, s_dv.v)
                sc.em("v", lambda v: v.tensor_copy(
                    sc_s[:, 11:12], iscr[:, 0:1]).then_inc(s_dv.h, 1))
                s_dv.v += 1
                sc.w("v", s_dv, s_dv.v)
                sc.em("v", lambda v: v.tensor_scalar(
                    sc_s[:, 12:13], sc_s[:, 11:12], -128.0, None,
                    ALU.add).then_inc(s_dv.h, 1))
                s_dv.v += 1
                sc.w("v", s_dv, s_dv.v)
                sc.em("v", lambda v: v.tensor_scalar(
                    sc_s[:, 12:13], sc_s[:, 12:13], -128.0, 127.0,
                    ALU.max, ALU.min).then_inc(s_dv.h, 1))
                s_dv.v += 1
                sc.w("v", s_dv, s_dv.v)
                sc.em("v", lambda v: v.tensor_tensor(
                    sc_s[:, 13:14], sc_s[:, 9:10], sc_s[0:1, 2:3],
                    ALU.mult).then_inc(s_dv.h, 1))
                s_dv.v += 1
                sc.w("v", s_dv, s_dv.v)
                sc.em("v", lambda v: v.tensor_tensor(
                    sc_s[:, 14:15], sc_s[:, 13:14], sc_s[:, 12:13],
                    ALU.mult).then_inc(s_dv.h, 1))
                s_dv.v += 1
                sc.w("v", s_dv, s_dv.v)
                sc.em("v", lambda v: v.tensor_scalar(
                    sc_s[:, 14:15], sc_s[:, 14:15], -1.0, None,
                    ALU.mult).then_inc(s_dv.h, 1))
                s_dv.v += 1
                sc.w("t", s_dv, s_dv.v)
                sc.em("t", lambda t: t.matmul(
                    psum[:, 0:5], ones_f[0:1, :], sc_s[0:1, 10:15],
                    start=True, stop=True).then_inc(s_pe.h, 1))
                s_pe.v += 1
                sc.w("v", s_pe, s_pe.v)

                def v_cols(v):
                    v.tensor_copy(eff_s[:, 4:5], psum[:, 0:1])
                    v.tensor_copy(eff_s[:, 5:6], psum[:, 2:3])
                    v.tensor_copy(eff_s[:, 0:1], psum[:, 3:4])
                    v.tensor_copy(eff_s[:, 6:7], psum[:, 4:5])\
                        .then_inc(s_dv.h, 1)
                sc.em("v", v_cols)
                s_dv.v += 1
                sc.w("v", s_dv, s_dv.v)
                sc.em("v", lambda v: v.tensor_scalar(
                    q1s_s[:], inv2_s[:], eff_s[:, 4:5], None,
                    ALU.mult).then_inc(s_dv.h, 1))
                s_dv.v += 1
                sc.w("v", s_dv, s_dv.v)
                sc.em("v", lambda v: v.tensor_tensor(
                    q1b_s[:], m2_s[:], q1s_s[:], ALU.mult).then_inc(s_dv.h, 1))
                s_dv.v += 1
                sc.w("v", s_dv, s_dv.v)
                sc.em("v", lambda v: v.tensor_scalar(
                    q1b_s[:], q1b_s[:], -1.0, eff_s[:, 5:6],
                    ALU.mult, ALU.add).then_inc(s_dv.h, 1))
                s_dv.v += 1
                sc.w("v", s_dv, s_dv.v)
                sc.em("v", lambda v: v.scalar_tensor_tensor(
                    b1mod_s[:], colsum1_s[:], eff_s[:, 6:7], b1col_s[:],
                    ALU.mult, ALU.add).then_inc(s_dv.h, 1))
                s_dv.v += 1
            collective(sc, cc1_in, cc1_out, after1)
            barrier(sc)
            run_phase(sc)

        if upto in ("P2", "P3"):
            with ExitStack() as ps:
                xq1_fm = sb("xq1_fm", [128, 8 * T], BF16, ps)
                xm2 = [sb(f"xm2_{k}", [128, 4 * D], F32, ps)
                       for k in range(2)]
                qtmp = [sb(f"qtmp{k}", [128, D], F32, ps)
                        for k in range(2)]
                qint = sb("qint", [128, D], I32, ps)
                xqtm = [sb(f"xqtm{k}", [128, D], BF16, ps)
                        for k in range(2)]
                w1r = [sb(f"w1r{k}", [128, 1024], BF16, ps)
                       for k in range(2)]
                hrow = [sb(f"hrow{k}", [128, HF], F32, ps)
                        for k in range(2)]

                sc = Sched()
                ldm = {}
                acq = {}
                dvqx = {}
                dvq2 = {}
                petr = {}

                def emit_ldm(i):
                    if i >= 2:
                        sc.w("y", s_dv, dvq2[4 * (i - 2) + 3])
                    sc.em("y", lambda y, i=i: y.dma_start(
                        out=xm2[i % 2][:].rearrange(
                            "p (c d) -> p c d", d=D),
                        in_=xmid_scr[i * TT:(i + 1) * TT, :].rearrange(
                            "(c p) d -> p c d", p=128))
                        .then_inc(d_u.h, 16))
                    d_u.v += 16
                    ldm[i] = d_u.v

                def emit_qchunk(c):
                    i, r = c // 4, c % 4
                    sc.w("s", d_u, ldm[i])
                    if c >= 2:
                        sc.w("s", s_dv, dvqx[c - 2])
                    sc.em("s", lambda s, i=i, r=r, c=c: s.activation(
                        qtmp[c % 2][:], xm2[i % 2][:, r * D:(r + 1) * D],
                        AF.Identity, scale=q1s_s[:, c:c + 1],
                        bias=q1b_s[:, c:c + 1]).then_inc(s_ac.h, 1))
                    s_ac.v += 1
                    acq[c] = s_ac.v
                    sc.w("v", s_ac, acq[c])
                    if c >= 2:
                        sc.w("v", s_pe, petr[c - 2])

                    def v_qx(v, c=c):
                        v.tensor_copy(qint[:], qtmp[c % 2][:])
                        _li = v.tensor_scalar(xqtm[c % 2][:], qint[:],
                                              clip_lo[:], clip_hi[:],
                                              ALU.max, ALU.min)
                        _li.then_inc(s_dv.h, 1)
                    sc.em("v", v_qx)
                    s_dv.v += 1
                    dvqx[c] = s_dv.v
                    sc.w("t", s_dv, dvqx[c])
                    if c >= 2:
                        sc.w("t", s_dv, dvq2[c - 2])

                    def t_q(t, c=c):
                        pbb = pk(2048 + (c % 2) * 512).bitcast(BF16)
                        for dd in range(8):
                            _li = t.transpose(
                                pbb[:, dd * 128:(dd + 1) * 128],
                                xqtm[c % 2][:, dd * 128:(dd + 1) * 128],
                                ident_b[:])
                        _li.then_inc(s_pe.h, 1)
                    sc.em("t", t_q)
                    s_pe.v += 1
                    petr[c] = s_pe.v
                    sc.w("v", s_pe, petr[c])

                    def v_qev(v, c=c):
                        pbb = pk(2048 + (c % 2) * 512).bitcast(BF16)
                        dst = xq1_fm[:].rearrange(
                            "p (d t) -> p d t",
                            t=T)[:, :, c * 128:(c + 1) * 128]
                        _li = v.tensor_copy(dst, pbb[:].rearrange(
                            "p (d t) -> p d t", t=128))
                        _li.then_inc(s_dv.h, 1)
                    sc.em("v", v_qev)
                    s_dv.v += 1
                    dvq2[c] = s_dv.v
                    if r == 3 and i + 2 < NTT:
                        emit_ldm(i + 2)

                emit_ldm(0)
                emit_ldm(1)
                for c in range(4):
                    emit_qchunk(c)

                wld = {}
                ach = {}
                acseg = {}
                schw = {}
                for j in range(32):
                    if j >= 2:
                        sc.w("y", s_ac, ach[j - 2])
                    sc.em("y", lambda y, j=j: y.dma_start(
                        out=w1r[j % 2][:],
                        in_=w1q_scr[j].rearrange("p d c -> p (d c)"))
                        .then_inc(d_w.h, 16))
                    d_w.v += 16
                    wld[j] = d_w.v
                    sc.w("t", d_w, wld[j])
                    if j >= 2:
                        sc.w("s", d_sc, schw[j - 2])
                    for ts_ in range(8):
                        gseg = 8 * j + ts_
                        if gseg >= 2:
                            sc.w("t", s_ac, acseg[gseg - 2])
                        if j == 0:
                            sc.w("t", s_dv, dvq2[4 * ts_ + 3])

                        def t_f1(t, j=j, ts_=ts_):
                            pb = pk((ts_ % 2) * 512)
                            for dd in range(8):
                                _li = t.matmul(
                                    pb,
                                    w1r[j % 2][:, dd * 128:(dd + 1) * 128],
                                    xq1_fm[:, dd * T + ts_ * 512:
                                           dd * T + (ts_ + 1) * 512],
                                    start=(dd == 0), stop=(dd == 7))
                            _li.then_inc(s_pe.h, 1)
                        sc.em("t", t_f1)
                        s_pe.v += 1
                        sc.w("s", s_pe, s_pe.v)
                        sc.em("s", lambda s, j=j, ts_=ts_: s.activation(
                            hrow[j % 2][:, ts_ * 512:(ts_ + 1) * 512],
                            pk((ts_ % 2) * 512), AF.Gelu,
                            scale=eff_s[:, 0:1],
                            bias=b1mod_s[:, j:j + 1]).then_inc(s_ac.h, 1))
                        s_ac.v += 1
                        acseg[gseg] = s_ac.v
                        if j == 0 and ts_ < 7:
                            for cq in range(4 * ts_ + 4, 4 * ts_ + 8):
                                emit_qchunk(cq)
                    ach[j] = s_ac.v
                    sc.w("v", s_ac, ach[j])

                    def v_hmm(v, j=j):
                        v.reduce_max(mmt2[:, j:j + 1], hrow[j % 2][:], AXX)
                        _li = v.tensor_reduce(mmt2[:, 32 + j:33 + j],
                                        hrow[j % 2][:], AXX, ALU.min)
                        _li.then_inc(s_dv.h, 1)
                    sc.em("v", v_hmm)
                    s_dv.v += 1
                    sc.w("g", s_ac, ach[j])
                    sc.em("g", lambda g, j=j: g.dma_start(
                        out=h_scr[j], in_=hrow[j % 2][:])
                        .then_inc(d_sc.h, 16))
                    d_sc.v += 16
                    schw[j] = d_sc.v

                def v_hm(v):
                    v.reduce_max(mmt2[:, 0:1], mmt2[:, 0:32], AXX)
                    v.tensor_reduce(mmt2[:, 1:2], mmt2[:, 32:64], AXX,
                                    ALU.min).then_inc(s_dv.h, 1)
                sc.em("v", v_hm)
                s_dv.v += 1
                sc.w("v", s_dv, s_dv.v)
                sc.em("v", lambda v: v.tensor_scalar(
                    mmt2[:, 1:2], mmt2[:, 1:2], -1.0, None, ALU.mult)
                    .then_inc(s_dv.h, 1))
                s_dv.v += 1
                sc.w("t", s_dv, s_dv.v)
                def t_trm2(t):
                    t.transpose(psum[0:1, 0:128], mmt2[:, 0:1], identf[:])
                    _li = t.transpose(psum[0:1, 128:256], mmt2[:, 1:2], identf[:])
                    _li.then_inc(s_pe.h, 1)
                sc.em("t", t_trm2)
                s_pe.v += 1
                sc.w("v", s_pe, s_pe.v)

                def v_hm2(v):
                    v.reduce_max(mmrow[:, 0:1], psum[0:1, 0:128], AXX)
                    _li = v.reduce_max(mmrow[:, 1:2], psum[0:1, 128:256], AXX)
                    _li.then_inc(s_dv.h, 1)
                sc.em("v", v_hm2)
                s_dv.v += 1
                barrier(sc)
                run_phase(sc)

        if upto == "P3":
            sc = Sched()

            def after2(sc):
                sc.em("v", lambda v: v.tensor_tensor(
                    sc_s[:, 16:17], ccr[:, 0:1], ccr[:, 1:2],
                    ALU.add).then_inc(s_dv.h, 1))
                s_dv.v += 1
                sc.w("v", s_dv, s_dv.v)
                sc.em("v", lambda v: v.tensor_scalar(
                    sc_s[:, 17:18], sc_s[:, 16:17], 1.0 / 255.0, 1e-8,
                    ALU.mult, ALU.max).then_inc(s_dv.h, 1))
                s_dv.v += 1
                sc.w("v", s_dv, s_dv.v)
                sc.em("v", lambda v: v.reciprocal(
                    sc_s[:, 18:19], sc_s[:, 17:18]).then_inc(s_dv.h, 1))
                s_dv.v += 1
                sc.w("v", s_dv, s_dv.v)
                sc.em("v", lambda v: v.tensor_tensor(
                    sc_s[:, 19:20], ccr[:, 1:2], sc_s[:, 18:19],
                    ALU.mult).then_inc(s_dv.h, 1))
                s_dv.v += 1
                iscr2 = sb("cc_i322", [1, 2], I32)
                sc.w("v", s_dv, s_dv.v)

                sc.em("v", lambda v: v.tensor_copy(iscr2[:, 0:1], sc_s[:, 19:20]).then_inc(s_dv.h, 1))
                s_dv.v += 1
                sc.w("v", s_dv, s_dv.v)
                sc.em("v", lambda v: v.tensor_copy(sc_s[:, 19:20], iscr2[:, 0:1]).then_inc(s_dv.h, 1))
                s_dv.v += 1
                sc.w("v", s_dv, s_dv.v)
                sc.em("v", lambda v: v.tensor_scalar(sc_s[:, 20:21], sc_s[:, 19:20], -128.0,
                    None, ALU.add).then_inc(s_dv.h, 1))
                s_dv.v += 1
                sc.w("v", s_dv, s_dv.v)
                sc.em("v", lambda v: v.tensor_scalar(sc_s[:, 20:21], sc_s[:, 20:21], -128.0,
                    127.0, ALU.max, ALU.min).then_inc(s_dv.h, 1))
                s_dv.v += 1
                sc.w("v", s_dv, s_dv.v)
                sc.em("v", lambda v: v.tensor_tensor(sc_s[:, 21:22], sc_s[:, 17:18],
                    sc_s[0:1, 3:4], ALU.mult).then_inc(s_dv.h, 1))
                s_dv.v += 1
                sc.w("v", s_dv, s_dv.v)
                sc.em("v", lambda v: v.tensor_tensor(sc_s[:, 23:24], sc_s[:, 20:21],
                    sc_s[:, 21:22], ALU.mult).then_inc(s_dv.h, 1))
                s_dv.v += 1
                sc.w("v", s_dv, s_dv.v)
                sc.em("v", lambda v: v.tensor_scalar(sc_s[:, 23:24], sc_s[:, 23:24], -1.0,
                    None, ALU.mult).then_inc(s_dv.h, 1))
                s_dv.v += 1
                sc.w("v", s_dv, s_dv.v)
                sc.em("v", lambda v: v.scalar_tensor_tensor(b2mod_s[:], colsum2_s[:],
                    sc_s[0:1, 23:24], b2row_s[:],
                    ALU.mult, ALU.add).then_inc(s_dv.h, 1))
                s_dv.v += 1
                sc.w("t", s_dv, s_dv.v)
                sc.em("t", lambda t: t.matmul(
                    psum[:, 0:4], ones_f[0:1, :], sc_s[0:1, 18:22],
                    start=True, stop=True).then_inc(s_pe.h, 1))
                s_pe.v += 1
                sc.w("v", s_pe, s_pe.v)

                def v_cols(v):
                    v.tensor_copy(eff_s[:, 2:3], psum[:, 0:1])
                    v.tensor_copy(eff_s[:, 3:4], psum[:, 2:3])
                    _li = v.tensor_copy(eff_s[:, 1:2], psum[:, 3:4])
                    _li.then_inc(s_dv.h, 1)
                sc.em("v", v_cols)
                s_dv.v += 1
                sc.w("t", s_dv, s_dv.v)
                def t_b2bc(t):
                    for oh in range(2):
                        _li = t.matmul(pk(512 + oh * 512),
                                       ones_f[0:1, :],
                                       b2mod_s[:, oh * 512:(oh + 1) * 512],
                                       start=True, stop=True)
                    _li.then_inc(s_pe.h, 1)
                sc.em("t", t_b2bc)
                s_pe.v += 1
                sc.w("v", s_pe, s_pe.v)
                sc.em("v", lambda v: v.tensor_copy(
                    b2bc[:], psum[:, 512:1536]).then_inc(s_dv.h, 1))
                s_dv.v += 1
            collective(sc, cc2_in, cc2_out, after2)
            barrier(sc)
            run_phase(sc)

        if upto == "P3":
            with ExitStack() as ps:
                w2sb = sb("w2sb", [128, 32 * D], BF16, ps)
                hgrp = [sb(f"hgrp{k}", [128, 32 * 128], F32, ps)
                        for k in range(3)]
                q3t = sb("q3t", [128, 32 * 128], F32, ps)
                q3i = sb("q3i", [128, 32 * 128], I32, ps)
                hq = [sb(f"hq{k}", [128, 32 * 128], BF16, ps)
                      for k in range(2)]
                xm3 = [sb(f"xm3_{k}", [128, D], F32, ps)
                       for k in range(2)]
                ob = [sb(f"ob{k}", [128, D], F32, ps) for k in range(2)]

                sc = Sched()
                for k in range(32):
                    sc.em("y", lambda y, k=k: y.dma_start(
                        out=w2sb[:, k * D:(k + 1) * D], in_=w2q_scr[k])
                        .then_inc(d_w.h, 16))
                    d_w.v += 16
                w2ld = d_w.v
                sc.w("t", d_w, w2ld)

                ldh = {}
                ldx = {}
                dvh = {}
                obm = {}
                outm = {}
                qtm = {}

                def emit_ldh(cc):
                    if cc >= 3:
                        sc.w("y", s_ac, qtm[cc - 3])
                    for part in range(16):
                        sc.em("y", lambda y, cc=cc, part=part: y.dma_start(
                            out=hgrp[cc % 3][:].rearrange(
                                "p (j t) -> p j t",
                                t=128)[:, part * 2:(part + 1) * 2],
                            in_=h_scr[part * 2:(part + 1) * 2, :,
                                      cc * 128:(cc + 1) * 128]
                            .transpose([1, 0, 2]))
                            .then_inc(d_u.h, 16))
                        d_u.v += 16
                    ldh[cc] = d_u.v

                emit_ldh(0)
                emit_ldh(1)
                for c in range(32):
                    if c + 2 <= 31:
                        emit_ldh(c + 2)
                    if c >= 2:
                        sc.w("y", s_dv, outm[c - 2])
                    sc.em("y", lambda y, c=c: y.dma_start(
                        out=xm3[c % 2][:],
                        in_=xmid_scr[c * 128:(c + 1) * 128, :])
                        .then_inc(d_u.h, 16))
                    d_u.v += 16
                    ldx[c] = d_u.v

                    sc.w("s", d_u, ldh[c])
                    if c >= 1:
                        sc.w("s", s_dv, dvh[c - 1])
                    sc.em("s", lambda s, c=c: s.activation(
                        q3t[:], hgrp[c % 3][:], AF.Identity,
                        scale=eff_s[:, 2:3], bias=eff_s[:, 3:4])
                        .then_inc(s_ac.h, 1))
                    s_ac.v += 1
                    qtm[c] = s_ac.v
                    sc.w("v", s_ac, s_ac.v)
                    sc.w("v", d_u, ldx[c])
                    if c >= 2:
                        sc.w("v", s_pe, obm[c - 2])

                    def v_q3(v, c=c):
                        v.tensor_copy(q3i[:], q3t[:])
                        v.tensor_scalar(hq[c % 2][:], q3i[:],
                                        clip_lo[:], clip_hi[:],
                                        ALU.max, ALU.min)
                        _li = v.tensor_tensor(xm3[c % 2][:], xm3[c % 2][:],
                                        b2bc[:], ALU.add)
                        _li.then_inc(s_dv.h, 1)
                    sc.em("v", v_q3)
                    s_dv.v += 1
                    dvh[c] = s_dv.v

                    sc.w("t", s_dv, dvh[c])
                    if c >= 2:
                        sc.w("t", s_dv, outm[c - 2])

                    def t_f2(t, c=c):
                        for oh in range(2):
                            pb = pk((c % 2) * 1024 + oh * 512)
                            for k in range(32):
                                _li = t.matmul(
                                    pb,
                                    hq[c % 2][:, k * 128:(k + 1) * 128],
                                    w2sb[:, k * D + oh * 512:
                                         k * D + (oh + 1) * 512],
                                    start=(k == 0), stop=(k == 31))
                        _li.then_inc(s_pe.h, 1)
                    sc.em("t", t_f2)
                    s_pe.v += 1
                    obm[c] = s_pe.v

                    def emit_vout(cc):
                        sc.w("v", s_pe, obm[cc])
                        if cc >= 2:
                            sc.w("v", d_out, 16 * (cc - 1))

                        def v_out(v, cc=cc):
                            for oh in range(2):
                                _li = v.scalar_tensor_tensor(
                                    ob[cc % 2][:, oh * 512:(oh + 1) * 512],
                                    pk((cc % 2) * 1024 + oh * 512),
                                    eff_s[:, 1:2],
                                    xm3[cc % 2][:, oh * 512:(oh + 1) * 512],
                                    ALU.mult, ALU.add)
                            _li.then_inc(s_dv.h, 1)
                        sc.em("v", v_out)
                        s_dv.v += 1
                        outm[cc] = s_dv.v
                        sc.w("g", s_dv, outm[cc])
                        sc.em("g", lambda g, cc=cc: g.dma_start(
                            out=out_d[cc * 128:(cc + 1) * 128, :],
                            in_=ob[cc % 2][:]).then_inc(d_out.h, 16))
                        d_out.v += 16

                    if c >= 1:
                        emit_vout(c - 1)
                    if c == 31:
                        emit_vout(31)
                barrier(sc)
                run_phase(sc)

        sc = Sched()
        sc.w("g", d_sc, d_sc.v)
        sc.w("g", d_out, d_out.v)
        run_phase(sc)

        res = run_bass_kernel_spmd(nc, in_maps,
                                   core_ids=list(range(len(in_maps))))
    return res


def _prep_inputs(inputs):
    u = np.ascontiguousarray(np.asarray(inputs["L0_attn_u"], np.float32))
    ipw = np.asarray(inputs["L0_attn_in_proj_w"], dtype=np.float32)
    winT = np.ascontiguousarray(ipw.T)
    cw = np.asarray(inputs["L0_attn_conv_w"], dtype=np.float32)
    cb = np.asarray(inputs["L0_attn_conv_b"], dtype=np.float32)
    convw = np.zeros((128, 136), np.float32)
    convb = np.zeros((128, 34), np.float32)
    for ct in range(34):
        n = 128 if ct < 33 else 32
        convw[0:n, ct * 4:(ct + 1) * 4] = cw[ct * 128:ct * 128 + n, :]
        convb[0:n, ct] = cb[ct * 128:ct * 128 + n]
    alog = np.ascontiguousarray(
        np.asarray(inputs["L0_attn_A_log"], np.float32).reshape(NH, 1))
    dtb = np.ascontiguousarray(
        np.asarray(inputs["L0_attn_dt_bias"], np.float32).reshape(NH, 1))
    drow = np.ascontiguousarray(
        np.asarray(inputs["L0_attn_Dparam"], np.float32).reshape(1, DI))
    nwcol = np.ascontiguousarray(
        np.asarray(inputs["L0_attn_norm_w"], np.float32).reshape(16, 128).T)
    woT = np.ascontiguousarray(
        np.asarray(inputs["L0_attn_out_proj_w"], np.float32).T)
    w1T = np.ascontiguousarray(np.asarray(inputs["L0_ffn_W1"], np.float32).T)
    b1col = np.ascontiguousarray(
        np.asarray(inputs["L0_ffn_b1"], np.float32).reshape(32, 128).T)
    w2T = np.ascontiguousarray(np.asarray(inputs["L0_ffn_W2"], np.float32).T)
    b2row = np.ascontiguousarray(
        np.asarray(inputs["L0_ffn_b2"], np.float32).reshape(1, D))
    shared = dict(winT=winT, convw=convw, convb=convb, alog=alog, dtb=dtb,
                  drow=drow, nwcol=nwcol, woT=woT, w1T=w1T, b1col=b1col,
                  w2T=w2T, b2row=b2row)
    in_maps = []
    for i in range(u.shape[0]):
        m = dict(shared)
        m["u"] = np.ascontiguousarray(u[i])
        in_maps.append(m)
    return in_maps


def kernel(**inputs):
    in_maps = _prep_inputs(inputs)
    res = build_and_run(in_maps, upto="P3")
    return np.stack([np.asarray(r["out"]) for r in res.results], axis=0)


# version 15
# speedup vs baseline: 1.1340x; 1.1340x over previous
import numpy as np
from contextlib import ExitStack
import concourse.bass as bass
import concourse.mybir as mybir
from concourse.bass_utils import run_bass_kernel_spmd

F32 = mybir.dt.float32
BF16 = mybir.dt.bfloat16
I32 = mybir.dt.int32
AF = mybir.ActivationFunctionType
ALU = mybir.AluOpType
AXX = mybir.AxisListType.X

T = 4096
D = 1024
DI = 2048
DP = 4256
HF = 4096
NH = 32
Q = 128
NCH = T // Q
TT = 512
NTT = T // TT
FP32_EPS = float(np.finfo(np.float32).eps)
LN_EPS = 1e-5
ENGINES = ("t", "v", "s", "g", "y")
P0_STAGE = 99


class Sem:
    def __init__(self, es, nc, name):
        self.h = es.enter_context(nc.semaphore(name))
        self.v = 0


class Sched:
    def __init__(self):
        self.st = {e: [] for e in ENGINES}

    def em(self, e, fn):
        self.st[e].append(fn)

    def w(self, e, sem, val):
        if val > 0:
            self.em(e, lambda eng, s=sem.h, v=val: eng.wait_ge(s, v))

    def run(self, block):
        st = self.st

        @block.tensor
        def _(t):
            for f in st["t"]:
                f(t)

        @block.vector
        def _(v):
            for f in st["v"]:
                f(v)

        @block.scalar
        def _(s):
            for f in st["s"]:
                f(s)

        @block.gpsimd
        def _(g):
            for f in st["g"]:
                f(g)

        @block.sync
        def _(y):
            for f in st["y"]:
                f(y)


def _early_exit(nc, in_maps):
    res = run_bass_kernel_spmd(nc, in_maps, core_ids=list(range(len(in_maps))))
    return res


def build_and_run(in_maps, upto="P3", dbg=False):
    nc = bass.Bass("TRN2", target_bir_lowering=False,
                  detect_race_conditions=False)
    with ExitStack() as es:

        def din(name, shape, dt=F32):
            return nc.dram_tensor(name, shape, dt, kind="ExternalInput").ap()

        u_d = din("u", [T, D])
        winT_d = din("winT", [D, DP])
        convw_d = din("convw", [128, 136])
        convb_d = din("convb", [128, 34])
        alog_d = din("alog", [NH, 1])
        dtb_d = din("dtb", [NH, 1])
        drow_d = din("drow", [1, DI])
        nwcol_d = din("nwcol", [128, 16])
        woT_d = din("woT", [DI, D])
        w1T_d = din("w1T", [D, HF])
        b1col_d = din("b1col", [128, 32])
        w2T_d = din("w2T", [HF, D])
        b2row_d = din("b2row", [1, D])
        out_d = nc.dram_tensor("out", [T, D], F32, kind="ExternalOutput").ap()

        skind = "ExternalOutput" if dbg else "Internal"

        def scr(name, shape, dt):
            return nc.dram_tensor(name, shape, dt, kind=skind).ap()

        z_scr = scr("z_scr", [16, 128, T], BF16)
        x_scr = scr("x_scr", [16, 128, T], BF16)
        bc_scr = scr("bc_scr", [128, T], F32)
        dt_scr = scr("dt_scr", [NH, T], F32)
        xmid_scr = scr("xmid_scr", [T, D], F32)
        h_scr = scr("h_scr", [32, 128, T], F32)
        w1q_scr = scr("w1q_scr", [32, 128, 8, 128], BF16)
        w2q_scr = scr("w2q_scr", [32, 128, D], BF16)
        sdbg = scr("sdbg", [128, 64], F32)
        cc1_in = nc.dram_tensor("cc1_in", [1, 2], F32, kind="Internal")
        cc1_out = nc.dram_tensor("cc1_out", [1, 2], F32, kind="Internal")
        cc2_in = nc.dram_tensor("cc2_in", [1, 2], F32, kind="Internal")
        cc2_out = nc.dram_tensor("cc2_out", [1, 2], F32, kind="Internal")

        s_pe = Sem(es, nc, "s_pe")
        s_dv = Sem(es, nc, "s_dv")
        s_ac = Sem(es, nc, "s_ac")
        s_gp = Sem(es, nc, "s_gp")
        s_sy = Sem(es, nc, "s_sy")
        d_u = Sem(es, nc, "d_u")
        d_w = Sem(es, nc, "d_w")
        d_sc = Sem(es, nc, "d_sc")
        d_mv = Sem(es, nc, "d_mv")
        d_cc = Sem(es, nc, "d_cc")
        d_out = Sem(es, nc, "d_out")
        ESEM = {"t": s_pe, "v": s_dv, "s": s_ac, "g": s_gp, "y": s_sy}

        def sb(name, shape, dt, stack=None):
            return (stack or es).enter_context(
                nc.sbuf_tensor(name, shape, dt))

        ones_f = sb("ones_f", [128, 128], F32)
        tril_f = sb("tril_f", [128, 128], F32)
        ident_b = sb("ident_b", [128, 128], BF16)
        identf = sb("identf", [128, 128], F32)
        onecol_b = sb("onecol_b", [128, 1], BF16)
        convw_s = sb("convw_s", [128, 136], F32)
        convb_s = sb("convb_s", [128, 34], F32)
        alog_s = sb("alog_s", [NH, 1], F32)
        dtb_s = sb("dtb_s", [NH, 1], F32)
        nwcol_s = sb("nwcol_s", [128, 16], F32)
        b1col_s = sb("b1col_s", [128, 32], F32)
        b2row_s = sb("b2row_s", [1, D], F32)
        sc_s = sb("sc_s", [1, 64], F32)
        colsum1_s = sb("colsum1_s", [128, 32], F32)
        colsum2_s = sb("colsum2_s", [1, D], F32)
        rw_s = sb("rw_s", [128, 2], F32)
        m2_s = sb("m2_s", [128, NCH], F32)
        inv2_s = sb("inv2_s", [128, NCH], F32)
        mmt1 = sb("mmt1", [128, 64], F32)
        mmt2 = sb("mmt2", [128, 64], F32)
        q1s_s = sb("q1s_s", [128, NCH], F32)
        q1b_s = sb("q1b_s", [128, NCH], F32)
        b1mod_s = sb("b1mod_s", [128, 32], F32)
        eff_s = sb("eff_s", [128, 8], F32)
        b2mod_s = sb("b2mod_s", [1, D], F32)
        b2bc = sb("b2bc", [128, D], F32)
        clip_lo = sb("clip_lo", [128, 1], F32)
        clip_hi = sb("clip_hi", [128, 1], F32)
        cst_t = sb("cst_t", [128, 2], F32)
        nc.const_aps.register(F32, LN_EPS, cst_t[:, 0:1])
        nc.const_aps.register(F32, FP32_EPS, cst_t[:, 1:2])
        mmrow = sb("mmrow", [1, 2], F32)
        ccr = sb("ccr", [1, 2], F32)
        ms = es.enter_context(ExitStack())
        dbc_b = sb("dbc_b", [128, DI], BF16, ms)
        state0 = sb("state0", [64, 1024], F32, ms)
        state1 = sb("state1", [64, 1024], F32, ms)
        state0b = sb("state0b", [64, 1024], BF16, ms)
        state1b = sb("state1b", [64, 1024], BF16, ms)

        psum = es.enter_context(nc.psum_tensor("psum", [128, 4096], F32))

        def pk(c0, w=512):
            return psum[:, c0:c0 + w]

        def run_phase(sc):
            blk = nc.Block()
            blk.__enter__()
            sc.run(blk)
            blk.__exit__(None, None, None)

        def barrier(sc):
            marks = {e: ESEM[e].v for e in ENGINES}
            dmarks = [(d_u, d_u.v), (d_w, d_w.v), (d_sc, d_sc.v),
                      (d_mv, d_mv.v), (d_cc, d_cc.v)]
            for e in ENGINES:
                for e2 in ENGINES:
                    if e2 != e:
                        sc.w(e, ESEM[e2], marks[e2])
                for dm, dv in dmarks:
                    sc.w(e, dm, dv)

        with ExitStack() as winscope:
            win_bf = sb("win_bf", [128, 8 * DP], BF16, winscope)

            with ExitStack() as p0s:
                drow_s = sb("drow_s", [1, DI], F32, p0s)
                sc = Sched()

                def g_consts(g):
                    g.memset(ones_f[:], 1.0)
                    g.affine_select(tril_f[:], ones_f[:], [[1, 128]],
                                    ALU.is_ge, 0.0, base=0,
                                    channel_multiplier=-1)
                    g.affine_select(identf[:], ones_f[:], [[1, 128]],
                                    ALU.is_equal, 0.0, base=0,
                                    channel_multiplier=-1)
                    g.memset(clip_lo[:], -128.0)
                    g.memset(clip_hi[:], 127.0)
                    g.memset(state0[:], 0.0)
                    g.memset(state1[:], 0.0)
                    g.memset(state0b[:], 0.0)
                    g.memset(state1b[:], 0.0)
                    g.memset(sc_s[:], 0.0)
                    g.memset(colsum1_s[:], 0.0)
                    g.memset(cst_t[:, 0:1], LN_EPS)
                    _li = g.memset(cst_t[:, 1:2], FP32_EPS)
                    _li.then_inc(s_gp.h, 1)
                sc.em("g", g_consts)
                s_gp.v += 1
                gp_c = s_gp.v
                sc.w("v", s_gp, gp_c)

                def v_idc(v):
                    v.tensor_copy(ident_b[:], identf[:])
                    _li = v.tensor_copy(onecol_b[:], ones_f[:, 0:1])
                    _li.then_inc(s_dv.h, 1)
                sc.em("v", v_idc)
                s_dv.v += 1

                def y_small(y):
                    for dst, src in ((convw_s, convw_d), (convb_s, convb_d),
                                     (alog_s, alog_d), (dtb_s, dtb_d),
                                     (nwcol_s, nwcol_d), (b1col_s, b1col_d),
                                     (b2row_s, b2row_d), (drow_s, drow_d)):
                        y.dma_start(out=dst[:], in_=src).then_inc(d_u.h, 16)
                sc.em("y", y_small)
                d_u.v += 128
                if P0_STAGE < 2:
                    barrier(sc)
                    run_phase(sc)
                    raise SystemExit(_early_exit(nc, in_maps))
                sc.w("t", d_u, d_u.v)
                sc.w("t", s_gp, gp_c)

                def t_dbc(t):
                    for j in range(4):
                        _li = t.matmul(pk(j * 512), ones_f[0:1, :],
                                 drow_s[:, j * 512:(j + 1) * 512],
                                 start=True, stop=True)
                    _li.then_inc(s_pe.h, 1)
                sc.em("t", t_dbc)
                s_pe.v += 1
                sc.w("v", s_pe, s_pe.v)
                sc.em("v", lambda v: v.tensor_copy(
                    dbc_b[:], psum[:, 0:2048]).then_inc(s_dv.h, 1))
                s_dv.v += 1
                dv_dbc = s_dv.v

                if P0_STAGE < 3:
                    barrier(sc)
                    run_phase(sc)
                    raise SystemExit(_early_exit(nc, in_maps))
                for dtl in range(8):
                    sc.em("g", lambda g, dtl=dtl: g.dma_start(
                        out=win_bf[:, dtl * DP:(dtl + 1) * DP],
                        in_=winT_d[dtl * 128:(dtl + 1) * 128, :])
                        .then_inc(d_mv.h, 16))
                    d_mv.v += 16

                WW = 2048
                wrow = [sb(f"wrow{k}", [128, WW], F32, p0s) for k in range(2)]
                wsc = [sb(f"wsc{k}", [128, WW], F32, p0s) for k in range(2)]
                wqi = [sb(f"wqi{k}", [128, WW], I32, p0s) for k in range(2)]
                wqb = [sb(f"wqb{k}", [128, WW], BF16, p0s) for k in range(2)]
                maxc = sb("maxc", [128, 48], F32, p0s)

                pieces = []
                for r in range(8):
                    for hf in range(2):
                        pieces.append(("w1", r, hf,
                                       w1T_d[r * 128:(r + 1) * 128,
                                             hf * WW:(hf + 1) * WW], WW))
                for r in range(32):
                    pieces.append(("w2", r, 0,
                                   w2T_d[r * 128:(r + 1) * 128, :], D))

                if P0_STAGE < 4:
                    barrier(sc)
                    run_phase(sc)
                    raise SystemExit(_early_exit(nc, in_maps))
                dvm = {}
                for idx, (wn, r, hf, src, width) in enumerate(pieces):
                    b = idx % 2
                    if idx >= 2:
                        sc.w("y", s_dv, dvm[idx - 2])
                    sc.em("y", lambda y, b=b, src=src, width=width:
                          y.dma_start(out=wrow[b][:, 0:width], in_=src)
                          .then_inc(d_w.h, 16))
                    d_w.v += 16
                    sc.w("v", d_w, d_w.v)
                    col = idx if wn == "w1" else 16 + r
                    sc.em("v", lambda v, b=b, width=width, col=col:
                          v.reduce_max(maxc[:, col:col + 1],
                                       wrow[b][:, 0:width], AXX,
                                       apply_absolute_value=True)
                          .then_inc(s_dv.h, 1))
                    s_dv.v += 1
                    dvm[idx] = s_dv.v

                def v_comb(v):
                    v.reduce_max(maxc[:, 0:1], maxc[:, 0:16], AXX)
                    _li = v.reduce_max(maxc[:, 1:2], maxc[:, 16:48], AXX)
                    _li.then_inc(s_dv.h, 1)
                sc.em("v", v_comb)
                s_dv.v += 1
                sc.w("t", s_dv, s_dv.v)
                sc.w("t", s_dv, dv_dbc)
                def t_trm(t):
                    t.transpose(psum[0:1, 0:128], maxc[:, 0:1], identf[:])
                    _li = t.transpose(psum[0:1, 128:256], maxc[:, 1:2], identf[:])
                    _li.then_inc(s_pe.h, 1)
                sc.em("t", t_trm)
                s_pe.v += 1
                sc.w("v", s_pe, s_pe.v)

                def v_sw_a(v):
                    v.reduce_max(sc_s[:, 0:1], psum[0:1, 0:128], AXX)
                    v.reduce_max(sc_s[:, 1:2], psum[0:1, 128:256], AXX)\
                        .then_inc(s_dv.h, 1)
                sc.em("v", v_sw_a)
                s_dv.v += 1
                sc.w("v", s_dv, s_dv.v)
                sc.em("v", lambda v: v.tensor_scalar(
                    sc_s[:, 2:4], sc_s[:, 0:2], 1.0 / 127.0, 1e-8,
                    ALU.mult, ALU.max).then_inc(s_dv.h, 1))
                s_dv.v += 1
                sc.w("v", s_dv, s_dv.v)
                sc.em("v", lambda v: v.reciprocal(
                    sc_s[:, 4:6], sc_s[:, 2:4]).then_inc(s_dv.h, 1))
                s_dv.v += 1
                sc.w("t", s_dv, s_dv.v)
                sc.em("t", lambda t: t.matmul(
                    psum[:, 0:2], ones_f[0:1, :], sc_s[0:1, 4:6],
                    start=True, stop=True).then_inc(s_pe.h, 1))
                s_pe.v += 1
                sc.w("v", s_pe, s_pe.v)
                sc.em("v", lambda v: v.tensor_copy(rw_s[:], psum[:, 0:2])
                      .then_inc(s_dv.h, 1))
                s_dv.v += 1
                dv_rw = s_dv.v
                if dbg:
                    sc.w("g", s_dv, dv_rw)
                    sc.em("g", lambda g: g.dma_start(
                        out=sdbg[0:1, :], in_=sc_s[:]).then_inc(d_mv.h, 16))
                    sc.em("g", lambda g: g.dma_start(
                        out=sdbg[1:3, 0:2], in_=rw_s[0:2, :])
                        .then_inc(d_mv.h, 16))
                    sc.em("g", lambda g: g.dma_start(
                        out=sdbg[4:5, 0:48], in_=maxc[0:1, :])
                        .then_inc(d_mv.h, 16))
                    d_mv.v += 48

                if P0_STAGE < 5:
                    barrier(sc)
                    run_phase(sc)
                    raise SystemExit(_early_exit(nc, in_maps))
                dvq = {}
                peq = {}
                scw = {}
                csadd = {}
                for idx, (wn, r, hf, src, width) in enumerate(pieces):
                    b = idx % 2
                    if idx >= 2:
                        sc.w("y", s_dv, dvq[idx - 2])
                    sc.em("y", lambda y, b=b, src=src, width=width:
                          y.dma_start(out=wrow[b][:, 0:width], in_=src)
                          .then_inc(d_w.h, 16))
                    d_w.v += 16
                    sc.w("s", d_w, d_w.v)
                    if idx == 0:
                        sc.w("s", s_dv, dv_rw)
                    if idx >= 2:
                        sc.w("s", s_dv, dvq[idx - 2])
                    rwc = rw_s[:, 0:1] if wn == "w1" else rw_s[:, 1:2]
                    sc.em("s", lambda s, b=b, width=width, rwc=rwc:
                          s.activation(wsc[b][:, 0:width],
                                       wrow[b][:, 0:width], AF.Copy,
                                       scale=rwc).then_inc(s_ac.h, 1))
                    s_ac.v += 1
                    sc.w("v", s_ac, s_ac.v)
                    if idx >= 2:
                        sc.w("v", d_sc, scw[idx - 2])
                        sc.w("v", s_pe, peq[idx - 2])

                    def v_q(v, b=b, width=width):
                        v.tensor_copy(wqi[b][:, 0:width], wsc[b][:, 0:width])
                        _li = v.tensor_copy(wqb[b][:, 0:width], wqi[b][:, 0:width])
                        _li.then_inc(s_dv.h, 1)
                    sc.em("v", v_q)
                    s_dv.v += 1
                    dvq[idx] = s_dv.v
                    sc.w("g", s_dv, dvq[idx])
                    if wn == "w1":
                        def g_w1(g, b=b, r=r, hf=hf):
                            dst = w1q_scr[hf * 16:(hf + 1) * 16, :, r, :]
                            g.dma_start(
                                out=dst.transpose([1, 0, 2]),
                                in_=wqb[b][:].rearrange(
                                    "p (j c) -> p j c", c=128))\
                                .then_inc(d_sc.h, 16)
                        sc.em("g", g_w1)
                    else:
                        sc.em("g", lambda g, b=b, r=r: g.dma_start(
                            out=w2q_scr[r], in_=wqb[b][:, 0:D])
                            .then_inc(d_sc.h, 16))
                    d_sc.v += 16
                    scw[idx] = d_sc.v
                    sc.w("t", s_dv, dvq[idx])
                    if wn == "w1":
                        if idx >= 1:
                            sc.w("t", s_dv, csadd.get(idx - 1, 0))

                        def t_cs1(t, b=b, hf=hf):
                            for j in range(16):
                                _li = t.matmul(psum[:, 1024 + j:1025 + j],
                                         wqb[b][:, j * 128:(j + 1) * 128],
                                         onecol_b[:],
                                         start=True, stop=True)
                            _li.then_inc(s_pe.h, 1)
                        sc.em("t", t_cs1)
                        s_pe.v += 1
                        peq[idx] = s_pe.v
                        sc.w("v", s_pe, peq[idx])

                        def v_csadd(v, hf=hf):
                            _li = v.tensor_tensor(
                                colsum1_s[:, hf * 16:(hf + 1) * 16],
                                colsum1_s[:, hf * 16:(hf + 1) * 16],
                                psum[:, 1024:1040], ALU.add)
                            _li.then_inc(s_dv.h, 1)
                        sc.em("v", v_csadd)
                        s_dv.v += 1
                        csadd[idx] = s_dv.v
                        continue
                    else:
                        def t_cs2(t, b=b, r=r):
                            for j in range(2):
                                _li = t.matmul(psum[0:1, 1536 + j * 512:
                                              2048 + j * 512],
                                         onecol_b[:],
                                         wqb[b][:, j * 512:(j + 1) * 512],
                                         start=(r == 0), stop=(r == 31))
                            _li.then_inc(s_pe.h, 1)
                        sc.em("t", t_cs2)
                    s_pe.v += 1
                    peq[idx] = s_pe.v
                sc.w("v", s_pe, s_pe.v)

                def v_cs(v):
                    _li = v.tensor_copy(colsum2_s[:], psum[0:1, 1536:2560])
                    _li.then_inc(s_dv.h, 1)
                sc.em("v", v_cs)
                s_dv.v += 1

                barrier(sc)
                run_phase(sc)

            if upto in ("P1a", "P1c", "P2", "P3"):
                with ExitStack() as ps:
                    u_ring = [sb(f"u_ring{k}", [128, 4 * D], F32, ps)
                              for k in range(2)]
                    lnx_tm = [sb(f"lnx_tm{k}", [128, 4 * D], BF16, ps)
                              for k in range(2)]
                    lnx_fm = sb("lnx_fm", [128, 8 * TT], BF16, ps)
                    lnst = sb("lnst", [128, 16], F32, ps)
                    bn12 = sb("bn12", [128, 48], F32, ps)
                    mv2 = sb("mv2", [128, 8], F32, ps)
                    zx_ring = [sb(f"zxr{k}", [128, 515], BF16, ps)
                               for k in range(4)]
                    bcbuf = sb("bcbuf", [128, 515], F32, ps)
                    dtbuf = sb("dtbuf", [NH, 515], F32, ps)
                    hist_b = sb("hist_b", [128, 96], BF16, ps)
                    hist_f = sb("hist_f", [128, 6], F32, ps)
                    cbuf = [sb(f"cbuf{k}", [128, TT], BF16, ps)
                            for k in range(4)]
                    bcc = sb("bcc", [128, TT], F32, ps)
                    dtc = sb("dtc", [NH, TT], F32, ps)

                    sc = Sched()

                    def g_hist(g):
                        g.memset(hist_b[:], 0.0)
                        g.memset(hist_f[:], 0.0).then_inc(s_gp.h, 1)
                    sc.em("g", g_hist)
                    s_gp.v += 1
                    gp_h0 = s_gp.v
                    sc.w("s", s_gp, gp_h0)
                    sc.w("v", s_gp, gp_h0)

                    u_mark = {}
                    ac_lnx = {}
                    ac_sqrt1 = {}
                    pe_tr_last = {}
                    dv_fmx = {}
                    pe_ip = {}
                    ac_zx = {}
                    dv_bcdt = {}
                    dv_conv = {}
                    sc_out = {}
                    mview = mv2[:].rearrange("p (r two) -> p two r", two=2)

                    for i in range(NTT):
                        if i >= 2:
                            sc.w("y", s_ac, ac_lnx[i - 2])
                        sc.em("y", lambda y, i=i: y.dma_start(
                            out=u_ring[i % 2][:].rearrange(
                                "p (c d) -> p c d", d=D),
                            in_=u_d[i * TT:(i + 1) * TT, :].rearrange(
                                "(c p) d -> p c d", p=128))
                            .then_inc(d_u.h, 16))
                        d_u.v += 16
                        u_mark[i] = d_u.v

                        sc.w("v", d_u, u_mark[i])
                        if i >= 1:
                            sc.w("v", s_ac, ac_sqrt1[i - 1])

                        def v_stats(v, i=i):
                            ur = u_ring[i % 2]
                            for r in range(4):
                                v.bn_stats(bn12[:, r * 12:r * 12 + 6],
                                           ur[:, r * D:r * D + 512])
                                v.bn_stats(bn12[:, r * 12 + 6:r * 12 + 12],
                                           ur[:, r * D + 512:(r + 1) * D])
                            for r in range(4):
                                _li = v.bn_aggr(mv2[:, 2 * r:2 * r + 2],
                                                bn12[:, r * 12:r * 12 + 12])
                            _li.then_inc(s_dv.h, 1)
                        sc.em("v", v_stats)
                        s_dv.v += 1
                        sc.w("s", s_dv, s_dv.v)
                        sc.em("s", lambda s: s.activation(
                            lnst[:, 8:12], mview[:, 1, :], AF.Sqrt,
                            bias=LN_EPS).then_inc(s_ac.h, 1))
                        s_ac.v += 1
                        ac_sqrt1[i] = s_ac.v
                        sc.w("v", s_ac, s_ac.v)

                        def v_inv(v):
                            v.reciprocal(lnst[:, 8:12], lnst[:, 8:12])\
                                .then_inc(s_dv.h, 1)
                        sc.em("v", v_inv)
                        s_dv.v += 1
                        sc.w("v", s_dv, s_dv.v)
                        sc.em("v", lambda v: v.scalar_tensor_tensor(
                            lnst[:, 12:16], mview[:, 0, :], -1.0,
                            lnst[:, 8:12], ALU.mult, ALU.mult)
                            .then_inc(s_dv.h, 1))
                        s_dv.v += 1
                        sc.w("s", s_dv, s_dv.v)
                        if i >= 2:
                            sc.w("s", s_pe, pe_tr_last.get(i - 2, 0))

                        def s_apply(s, i=i):
                            ur = u_ring[i % 2]
                            lt = lnx_tm[i % 2]
                            for r in range(4):
                                _li = s.activation(lt[:, r * D:(r + 1) * D],
                                             ur[:, r * D:(r + 1) * D],
                                             AF.Identity,
                                             scale=lnst[:, 8 + r:9 + r],
                                             bias=lnst[:, 12 + r:13 + r])
                            _li.then_inc(s_ac.h, 1)
                        sc.em("s", s_apply)
                        s_ac.v += 1
                        ac_lnx[i] = s_ac.v

                        sc.w("t", s_ac, ac_lnx[i])
                        for r in range(4):
                            gr = 4 * i + r
                            if gr >= 2:
                                sc.w("t", s_dv, dv_fmx[gr - 2])

                            def t_tr(t, i=i, r=r):
                                lt = lnx_tm[i % 2]
                                pbb = pk(1024 + (r % 2) * 512).bitcast(BF16)
                                for dd in range(8):
                                    _li = t.transpose(
                                        pbb[:, dd * 128:(dd + 1) * 128],
                                        lt[:, r * D + dd * 128:
                                           r * D + (dd + 1) * 128],
                                        ident_b[:])
                                _li.then_inc(s_pe.h, 1)
                            sc.em("t", t_tr)
                            s_pe.v += 1
                            pe_tr_last[i] = s_pe.v
                            sc.w("v", s_pe, s_pe.v)

                            def v_fmx(v, r=r):
                                pbb = pk(1024 + (r % 2) * 512).bitcast(BF16)
                                dst = lnx_fm[:].rearrange(
                                    "p (d c) -> p d c",
                                    c=512)[:, :, r * 128:(r + 1) * 128]
                                _li = v.tensor_copy(dst, pbb[:].rearrange(
                                    "p (d c) -> p d c", c=128))
                                _li.then_inc(s_dv.h, 1)
                            sc.em("v", v_fmx)
                            s_dv.v += 1
                            dv_fmx[gr] = s_dv.v

                        sc.w("t", s_dv, dv_fmx[4 * i + 3])
                        for ct in range(34):
                            if ct >= 2:
                                if ct - 2 < 32:
                                    sc.w("t", s_ac, ac_zx[(i, ct - 2)])
                                else:
                                    sc.w("t", s_dv, dv_bcdt[(i, ct - 2)])
                            elif i > 0:
                                sc.w("t", s_dv, dv_bcdt[(i - 1, ct + 32)])

                            def t_ip(t, ct=ct):
                                cw_ = 128 if ct < 33 else 32
                                pb = pk((ct % 2) * 512)[0:cw_, :]
                                for dd in range(8):
                                    _li = t.matmul(
                                        pb,
                                        win_bf[:, dd * DP + ct * 128:
                                               dd * DP + ct * 128 + cw_],
                                        lnx_fm[:, dd * 512:(dd + 1) * 512],
                                        start=(dd == 0), stop=(dd == 7))
                                _li.then_inc(s_pe.h, 1)
                            sc.em("t", t_ip)
                            s_pe.v += 1
                            pe_ip[(i, ct)] = s_pe.v

                            if ct < 32:
                                slot = zx_ring[ct % 4]
                                sc.w("s", s_pe, pe_ip[(i, ct)])
                                key = ((i, ct - 4) if ct >= 4
                                       else (i - 1, ct + 28))
                                if key in dv_conv:
                                    sc.w("s", s_dv, dv_conv[key])

                                def s_zx(s, ct=ct, slot=slot):
                                    s.copy(slot[:, 0:3],
                                           hist_b[:, ct * 3:ct * 3 + 3])
                                    s.copy(slot[:, 3:515],
                                           pk((ct % 2) * 512))
                                    _li = s.copy(hist_b[:, ct * 3:ct * 3 + 3],
                                           pk((ct % 2) * 512)[:, 509:512])
                                    _li.then_inc(s_ac.h, 1)
                                sc.em("s", s_zx)
                                s_ac.v += 1
                                ac_zx[(i, ct)] = s_ac.v

                                sc.w("v", s_ac, ac_zx[(i, ct)])
                                if key in sc_out:
                                    sc.w("v", d_sc, sc_out[key])

                                def v_conv(v, ct=ct, slot=slot):
                                    cb = cbuf[ct % 4]
                                    v.tensor_scalar(
                                        cb[:], slot[:, 0:512],
                                        convw_s[:, ct * 4:ct * 4 + 1],
                                        convb_s[:, ct:ct + 1],
                                        ALU.mult, ALU.add)
                                    for k2 in (1, 2, 3):
                                        _li = v.scalar_tensor_tensor(
                                            cb[:], slot[:, k2:512 + k2],
                                            convw_s[:, ct * 4 + k2:
                                                    ct * 4 + k2 + 1],
                                            cb[:], ALU.mult, ALU.add)
                                    _li.then_inc(s_dv.h, 1)
                                sc.em("v", v_conv)
                                s_dv.v += 1
                                dv_conv[(i, ct)] = s_dv.v
                                sc.w("g", s_dv, dv_conv[(i, ct)])
                                dstt = (z_scr[ct] if ct < 16
                                        else x_scr[ct - 16])

                                def g_out(g, i=i, ct=ct, dstt=dstt):
                                    g.dma_start(
                                        out=dstt[:, i * TT:(i + 1) * TT],
                                        in_=cbuf[ct % 4][:])\
                                        .then_inc(d_sc.h, 16)
                                sc.em("g", g_out)
                                d_sc.v += 16
                                sc_out[(i, ct)] = d_sc.v
                            else:
                                sc.w("v", s_pe, pe_ip[(i, ct)])
                                if i > 0:
                                    sc.w("v", d_sc, sc_out[(i - 1, ct)])
                                hoff = (ct - 32) * 3

                                def v_bcdt(v, ct=ct, hoff=hoff):
                                    buf = bcbuf if ct == 32 else dtbuf
                                    rows = 128 if ct == 32 else NH
                                    ob = bcc if ct == 32 else dtc
                                    v.tensor_copy(
                                        buf[0:rows, 0:3],
                                        hist_f[0:rows, hoff:hoff + 3])
                                    v.tensor_copy(
                                        buf[0:rows, 3:515],
                                        pk((ct % 2) * 512)[0:rows, :])
                                    v.tensor_copy(
                                        hist_f[0:rows, hoff:hoff + 3],
                                        pk((ct % 2) * 512)[0:rows, 509:512])
                                    v.tensor_scalar(
                                        ob[0:rows, :], buf[0:rows, 0:512],
                                        convw_s[0:rows,
                                                ct * 4:ct * 4 + 1],
                                        convb_s[0:rows, ct:ct + 1],
                                        ALU.mult, ALU.add)
                                    for k2 in (1, 2, 3):
                                        _li = v.scalar_tensor_tensor(
                                            ob[0:rows, :],
                                            buf[0:rows, k2:512 + k2],
                                            convw_s[0:rows, ct * 4 + k2:
                                                    ct * 4 + k2 + 1],
                                            ob[0:rows, :],
                                            ALU.mult, ALU.add)
                                    _li.then_inc(s_dv.h, 1)
                                sc.em("v", v_bcdt)
                                s_dv.v += 1
                                dv_bcdt[(i, ct)] = s_dv.v
                                sc.w("g", s_dv, dv_bcdt[(i, ct)])
                                dstt = bc_scr if ct == 32 else dt_scr
                                rows = 128 if ct == 32 else NH
                                srcb = bcc if ct == 32 else dtc

                                def g_bc(g, i=i, dstt=dstt, rows=rows,
                                         srcb=srcb):
                                    g.dma_start(
                                        out=dstt[0:rows,
                                                 i * TT:(i + 1) * TT],
                                        in_=srcb[0:rows, :])\
                                        .then_inc(d_sc.h, 16)
                                sc.em("g", g_bc)
                                d_sc.v += 16
                                sc_out[(i, ct)] = d_sc.v
                    barrier(sc)
                    run_phase(sc)

        if upto in ("P1c", "P2", "P3"):
            with ExitStack() as ps:
                wo_bf = sb("wo_bf", [128, 16 * D], BF16, ps)
                z_t = [sb(f"z_t{k}", [128, 16 * TT], BF16, ps)
                       for k in range(2)]
                x_t = [sb(f"x_t{k}", [128, 16 * TT], BF16, ps)
                       for k in range(2)]
                bcg = [sb(f"bcg{k}", [128, TT], F32, ps) for k in range(2)]
                dtg = [sb(f"dtg{k}", [NH, TT], F32, ps) for k in range(2)]
                u2 = [sb(f"u2_{k}", [128, D], F32, ps) for k in range(2)]
                clo = sb("clo", [64, TT], F32, ps)
                Bbf = [sb(f"Bbf{k}", [64, TT], BF16, ps) for k in range(2)]
                Cbf = [sb(f"Cbf{k}", [64, TT], BF16, ps) for k in range(2)]
                dts = sb("dts", [NH, TT], F32, ps)
                af = sb("af", [NH, TT], F32, ps)
                sl4 = sb("sl4", [NH, 8], F32, ps)
                eslr = sb("eslr", [1, 128], F32, ps)
                ebc4 = [sb(f"ebc4_{k}", [64, 128], F32, ps)
                        for k in range(2)]
                a_tm4 = sb("a_tm4", [128, 128], F32, ps)
                dt_tm4 = sb("dt_tm4", [128, 128], F32, ps)
                s_tm4 = sb("s_tm4", [128, 128], F32, ps)
                es4 = [sb(f"es4_{k}", [128, 128], F32, ps)
                       for k in range(2)]
                emns4 = sb("emns4", [128, 128], F32, ps)
                scol4 = [sb(f"scol4_{k}", [128, 128], F32, ps)
                         for k in range(2)]
                B_tm4 = [sb(f"B_tm4_{k}", [128, 256], BF16, ps)
                         for k in range(2)]
                MT4b = [sb(f"MT4b_{k}", [128, 512], BF16, ps)
                        for k in range(2)]
                xmod = [sb(f"xmod{k}", [128, 2048], BF16, ps)
                        for k in range(2)]
                xD = [sb(f"xD{k}", [128, 2048], BF16, ps) for k in range(2)]
                y2b = sb("y2b", [128, 2048], F32, ps)
                y3b = sb("y3b", [128, 2048], BF16, ps)
                yg = sb("yg", [128, 2048], BF16, ps)
                ybn = sb("ybn", [128, 24], F32, ps)
                yms = sb("yms", [128, 4], F32, ps)
                xm = [sb(f"xm_{k}", [128, D], F32, ps) for k in range(2)]
                bnx = sb("bnx", [128, 12], F32, ps)
                mvx = sb("mvx", [128, 2], F32, ps)
                v2c = sb("v2c", [128, 8], F32, ps)
                iv4 = sb("iv4", [128, 4], F32, ps)
                stf = [state0, state1]
                stb = [state0b, state1b]

                sc = Sched()
                dv_m = {}
                pe_m = {}
                ac_m = {}
                gp_m = {}
                ld_tile = {}
                ld_clo = {}
                ld_u2 = {}
                wr_xm = {}

                def FEN():
                    sc.w("v", s_dv, s_dv.v)

                wo_done = {}
                for k in range(16):
                    if k >= 2:
                        sc.w("y", s_dv, wo_done[k - 2])
                    sc.em("y", lambda y, k=k: y.dma_start(
                        out=u2[k % 2][:],
                        in_=woT_d[k * 128:(k + 1) * 128, :])
                        .then_inc(d_w.h, 16))
                    d_w.v += 16
                    sc.w("v", d_w, d_w.v)
                    sc.em("v", lambda v, k=k: v.tensor_scalar(
                        wo_bf[:, k * D:(k + 1) * D], u2[k % 2][:],
                        nwcol_s[:, k:k + 1], None, ALU.mult)
                        .then_inc(s_dv.h, 1))
                    s_dv.v += 1
                    wo_done[k] = s_dv.v

                def emit_loads(j):
                    if j >= 2:
                        sc.w("y", s_dv, dv_m[("gate", 4 * (j - 2) + 3)])
                        sc.w("y", s_pe, pe_m[("xt", 4 * (j - 2) + 3, 1)])
                    for part in range(4):
                        for scr_t, dst_t in ((z_scr, z_t), (x_scr, x_t)):
                            sc.em("y", lambda y, j=j, part=part, scr_t=scr_t,
                                  dst_t=dst_t: y.dma_start(
                                      out=dst_t[j % 2][:].rearrange(
                                          "p (c t) -> p c t",
                                          t=TT)[:, part * 4:(part + 1) * 4],
                                      in_=scr_t[part * 4:(part + 1) * 4, :,
                                                j * TT:(j + 1) * TT]
                                      .transpose([1, 0, 2]))
                                  .then_inc(d_u.h, 16))
                            d_u.v += 16
                    if j >= 2:
                        sc.w("y", s_dv, dv_m[("prep", j - 2)])
                    sc.em("y", lambda y, j=j: y.dma_start(
                        out=bcg[j % 2][:],
                        in_=bc_scr[:, j * TT:(j + 1) * TT])
                        .then_inc(d_u.h, 16))
                    d_u.v += 16
                    sc.em("y", lambda y, j=j: y.dma_start(
                        out=dtg[j % 2][:],
                        in_=dt_scr[:, j * TT:(j + 1) * TT])
                        .then_inc(d_u.h, 16))
                    d_u.v += 16
                    ld_tile[j] = d_u.v

                def emit_clo(j):
                    sc.w("g", d_u, ld_tile[j])
                    if j >= 1:
                        sc.w("g", s_dv, dv_m[("prep", j - 1)])
                    sc.em("g", lambda g, j=j: g.dma_start(
                        out=clo[:], in_=bcg[j % 2][64:128, :])
                        .then_inc(d_mv.h, 16))
                    d_mv.v += 16
                    ld_clo[j] = d_mv.v

                def emit_dts(j):
                    sc.w("s", d_u, ld_tile[j])

                    def f(s, j=j):
                        s.activation(dts[:], dtg[j % 2][:], AF.Exp,
                                     bias=dtb_s[:])
                        _li = s.activation(dts[:], dts[:], AF.Ln, bias=1.0)
                        _li.then_inc(s_ac.h, 1)
                    sc.em("s", f)
                    s_ac.v += 1
                    ac_m[("dts", j)] = s_ac.v

                def emit_prep(j):
                    sc.w("v", s_ac, ac_m[("dts", j)])
                    sc.w("v", d_mv, ld_clo[j])

                    def f(v, j=j):
                        v.tensor_scalar(af[:], dts[:], alog_s[:], None,
                                        ALU.mult)
                        v.tensor_copy(Bbf[j % 2][:], bcg[j % 2][0:64, :])
                        v.tensor_copy(Cbf[j % 2][:], clo[:])
                        for q4 in range(4):
                            _li = v.reduce_sum(
                                sl4[:, q4:q4 + 1],
                                af[:, q4 * 128:(q4 + 1) * 128], AXX)
                        _li.then_inc(s_dv.h, 1)
                    sc.em("v", f)
                    s_dv.v += 1
                    dv_m[("prep", j)] = s_dv.v

                def emit_smalls_tr(j):
                    sc.w("t", s_dv, dv_m[("prep", j)])
                    if j >= 1:
                        sc.w("t", s_dv, dv_m[("ebc4", j - 1)])

                    def f(t, j=j):
                        for r4 in range(4):
                            t.transpose(psum[:, 2560 + r4 * 32:
                                             2592 + r4 * 32],
                                        af[0:NH, r4 * 128:(r4 + 1) * 128],
                                        identf[0:NH, 0:NH])
                            t.transpose(psum[:, 2688 + r4 * 32:
                                             2720 + r4 * 32],
                                        dts[0:NH, r4 * 128:(r4 + 1) * 128],
                                        identf[0:NH, 0:NH])
                            _li = t.transpose(
                                psum[:, 2816 + r4 * 32:2848 + r4 * 32]
                                .bitcast(BF16)[:, 0:64],
                                Bbf[j % 2][:, r4 * 128:(r4 + 1) * 128],
                                ident_b[0:64, 0:64])
                        _li.then_inc(s_pe.h, 1)
                    sc.em("t", f)
                    s_pe.v += 1
                    pe_m[("smtr", j)] = s_pe.v

                def emit_smalls_evac(j):
                    sc.w("v", s_pe, pe_m[("smtr", j)])

                    def f(v, j=j):
                        v.tensor_copy(a_tm4[:], psum[:, 2560:2688])
                        v.tensor_copy(dt_tm4[:], psum[:, 2688:2816])
                        _li = v.tensor_copy(
                            B_tm4[j % 2][:], psum[:, 2816:2944].bitcast(BF16))
                        _li.then_inc(s_dv.h, 1)
                    sc.em("v", f)
                    s_dv.v += 1
                    dv_m[("smev", j)] = s_dv.v

                def emit_sMM(j):
                    sc.w("t", s_dv, dv_m[("smev", j)])
                    sc.em("t", lambda t: t.matmul(
                        psum[:, 2944:3072], tril_f[:], a_tm4[:],
                        start=True, stop=True).then_inc(s_pe.h, 1))
                    s_pe.v += 1
                    pe_m[("sMM", j)] = s_pe.v

                def emit_s4evac(j):
                    sc.w("v", s_pe, pe_m[("sMM", j)])
                    sc.em("v", lambda v: v.tensor_copy(
                        s_tm4[:], psum[:, 2944:3072]).then_inc(s_dv.h, 1))
                    s_dv.v += 1
                    dv_m[("s4ev", j)] = s_dv.v

                def emit_es4(j):
                    sc.w("s", s_dv, dv_m[("s4ev", j)])

                    def f(s, j=j):
                        s.activation(es4[j % 2][:], s_tm4[:], AF.Exp)
                        _li = s.activation(emns4[:], s_tm4[:], AF.Exp,
                                           scale=-1.0)
                        _li.then_inc(s_ac.h, 1)
                    sc.em("s", f)
                    s_ac.v += 1
                    ac_m[("es4", j)] = s_ac.v

                def emit_expsl(j):
                    sc.w("s", s_dv, dv_m[("prep", j)])
                    sc.em("s", lambda s: s.activation(
                        sl4[:, 4:8], sl4[:, 0:4], AF.Exp)
                        .then_inc(s_ac.h, 1))
                    s_ac.v += 1
                    ac_m[("expsl", j)] = s_ac.v

                def emit_scol4(j):
                    sc.w("v", s_ac, ac_m[("es4", j)])
                    sc.em("v", lambda v, j=j: v.tensor_tensor(
                        scol4[j % 2][:], dt_tm4[:], emns4[:], ALU.mult)
                        .then_inc(s_dv.h, 1))
                    s_dv.v += 1
                    dv_m[("scol", j)] = s_dv.v

                def emit_cbt4(j):
                    sc.w("t", s_dv, dv_m[("prep", j)])
                    if j >= 1:
                        sc.w("t", s_dv, dv_m[("mt4", j - 1)])

                    def f(t, j=j):
                        for r4 in range(4):
                            _li = t.matmul(
                                psum[:, 2048 + r4 * 128:2176 + r4 * 128],
                                Bbf[j % 2][:, r4 * 128:(r4 + 1) * 128],
                                Cbf[j % 2][:, r4 * 128:(r4 + 1) * 128],
                                start=True, stop=True)
                        _li.then_inc(s_pe.h, 1)
                    sc.em("t", f)
                    s_pe.v += 1
                    pe_m[("cbt4", j)] = s_pe.v

                def emit_mt4(j):
                    sc.w("v", s_pe, pe_m[("cbt4", j)])
                    sc.em("v", lambda v, j=j: v.tensor_tensor(
                        MT4b[j % 2][:].rearrange("p (c t) -> p c t", t=128),
                        psum[:, 2048:2560].rearrange("p (c t) -> p c t",
                                                     t=128),
                        tril_f[:].unsqueeze(1).broadcast_to((128, 4, 128)),
                        ALU.mult).then_inc(s_dv.h, 1))
                    s_dv.v += 1
                    dv_m[("mt4", j)] = s_dv.v

                def emit_esltr(j):
                    sc.w("t", s_ac, ac_m[("expsl", j)])
                    sc.w("t", s_dv, dv_m[("smev", j)])

                    def f(t):
                        for r4 in range(4):
                            _li = t.transpose(
                                psum[0:1, 2560 + r4 * 32:2592 + r4 * 32],
                                sl4[0:NH, 4 + r4:5 + r4],
                                identf[0:NH, 0:NH])
                        _li.then_inc(s_pe.h, 1)
                    sc.em("t", f)
                    s_pe.v += 1
                    pe_m[("esltr", j)] = s_pe.v

                def emit_eslr(j):
                    sc.w("v", s_pe, pe_m[("esltr", j)])
                    sc.em("v", lambda v: v.tensor_copy(
                        eslr[:], psum[0:1, 2560:2688]).then_inc(s_dv.h, 1))
                    s_dv.v += 1
                    dv_m[("eslr", j)] = s_dv.v

                def emit_ebcmm(j):
                    sc.w("t", s_dv, dv_m[("eslr", j)])
                    sc.em("t", lambda t: t.matmul(
                        psum[0:64, 2688:2816], ones_f[0:1, 0:64], eslr[:],
                        start=True, stop=True).then_inc(s_pe.h, 1))
                    s_pe.v += 1
                    pe_m[("ebcmm", j)] = s_pe.v

                def emit_ebc4(j):
                    sc.w("v", s_pe, pe_m[("ebcmm", j)])
                    sc.em("v", lambda v, j=j: v.tensor_copy(
                        ebc4[j % 2][:], psum[0:64, 2688:2816])
                        .then_inc(s_dv.h, 1))
                    s_dv.v += 1
                    dv_m[("ebc4", j)] = s_dv.v

                def emit_silu(j):
                    sc.w("s", d_u, ld_tile[j])
                    sc.em("s", lambda s, j=j: s.activation(
                        z_t[j % 2][:], z_t[j % 2][:], AF.Silu)
                        .then_inc(s_ac.h, 1))
                    s_ac.v += 1
                    ac_m[("silu", j)] = s_ac.v

                def emit_u2(cc):
                    if cc < 2:
                        sc.w("y", s_dv, wo_done[14 + cc])
                    else:
                        sc.w("y", s_dv, dv_m[("xm", cc - 2)])
                    sc.em("y", lambda y, cc=cc: y.dma_start(
                        out=u2[cc % 2][:],
                        in_=u_d[cc * 128:(cc + 1) * 128, :])
                        .then_inc(d_u.h, 16))
                    d_u.v += 16
                    ld_u2[cc] = d_u.v

                def emit_xt(cc, hg):
                    i2, r4 = (cc // 4) % 2, cc % 4
                    if hg == 0:
                        if cc >= 2:
                            sc.w("t", s_dv, dv_m[("state", cc - 2, 1)])
                        if cc >= 1:
                            sc.w("t", s_dv, dv_m[("xmod", cc - 1, 1)])
                        sc.w("t", d_u, ld_tile[cc // 4])

                    def f(t, i2=i2, r4=r4, hg=hg):
                        pbb = psum[:, 3072 + hg * 512:3584 + hg * 512]\
                            .bitcast(BF16)
                        for k in range(8):
                            cx = hg * 8 + k
                            _li = t.transpose(
                                pbb[:, k * 128:(k + 1) * 128],
                                x_t[i2][:, cx * TT + r4 * 128:
                                        cx * TT + (r4 + 1) * 128],
                                ident_b[:])
                        _li.then_inc(s_pe.h, 1)
                    sc.em("t", f)
                    s_pe.v += 1
                    pe_m[("xt", cc, hg)] = s_pe.v

                def emit_xmod(cc, hg):
                    i2, r4 = (cc // 4) % 2, cc % 4
                    sc.w("v", s_pe, pe_m[("xt", cc, hg)])

                    def f(v, cc=cc, i2=i2, r4=r4, hg=hg):
                        pbb = psum[:, 3072 + hg * 512:3584 + hg * 512]\
                            .bitcast(BF16)
                        src3 = pbb[:].rearrange("p (h e) -> p h e", e=64)
                        sc3 = scol4[i2][:, r4 * 32 + hg * 16:
                                        r4 * 32 + (hg + 1) * 16]\
                            .unsqueeze(2).broadcast_to((128, 16, 64))
                        dst3 = xmod[cc % 2][:, hg * 1024:(hg + 1) * 1024]\
                            .rearrange("p (h e) -> p h e", e=64)
                        db3 = dbc_b[:, hg * 1024:(hg + 1) * 1024]\
                            .rearrange("p (h e) -> p h e", e=64)
                        dstD = xD[cc % 2][:, hg * 1024:(hg + 1) * 1024]\
                            .rearrange("p (h e) -> p h e", e=64)
                        v.tensor_tensor(dst3, src3, sc3, ALU.mult)
                        _li = v.tensor_tensor(dstD, src3, db3, ALU.mult)
                        _li.then_inc(s_dv.h, 1)
                    sc.em("v", f)
                    s_dv.v += 1
                    dv_m[("xmod", cc, hg)] = s_dv.v

                def emit_y(cc, hg):
                    i2, r4 = (cc // 4) % 2, cc % 4
                    base = hg * 1024
                    if hg == 0:
                        if cc >= 2:
                            sc.w("t", s_dv, dv_m[("xm", cc - 2)])
                        sc.w("t", s_dv, dv_m[("xmod", cc, 1)])
                    else:
                        if cc >= 1:
                            sc.w("t", s_dv, dv_m[("gate", cc - 1)])
                    if cc >= 1:
                        sc.w("t", s_gp, gp_m[("sttb", cc - 1, hg)])

                    def f(t, cc=cc, i2=i2, r4=r4, base=base, hg=hg):
                        mt = MT4b[i2][:, r4 * 128:(r4 + 1) * 128]
                        cb = Cbf[i2][:, r4 * 128:(r4 + 1) * 128]
                        for h2 in range(2):
                            t.matmul(pk(base + h2 * 512), mt,
                                     xmod[cc % 2][:, base + h2 * 512:
                                                  base + (h2 + 1) * 512],
                                     start=True, stop=False)
                        for h2 in range(2):
                            _li = t.matmul(
                                pk(base + h2 * 512), cb,
                                stb[hg][:, h2 * 512:(h2 + 1) * 512],
                                start=False, stop=True)
                        _li.then_inc(s_pe.h, 1)
                    sc.em("t", f)
                    s_pe.v += 1
                    pe_m[("y", cc, hg)] = s_pe.v

                def emit_y1(cc, hg):
                    i2, r4 = (cc // 4) % 2, cc % 4
                    base = hg * 1024
                    sc.w("v", s_pe, pe_m[("y", cc, hg)])

                    def f(v, i2=i2, r4=r4, base=base, hg=hg):
                        p3 = pk(base, 1024)[:].rearrange(
                            "p (h e) -> p h e", e=64)
                        e3 = es4[i2][:, r4 * 32 + hg * 16:
                                     r4 * 32 + (hg + 1) * 16]\
                            .unsqueeze(2).broadcast_to((128, 16, 64))
                        d3 = y2b[:, base:base + 1024].rearrange(
                            "p (h e) -> p h e", e=64)
                        _li = v.tensor_tensor(d3, p3, e3, ALU.mult)
                        _li.then_inc(s_dv.h, 1)
                    sc.em("v", f)
                    s_dv.v += 1
                    dv_m[("y1", cc, hg)] = s_dv.v

                def emit_dh(cc, hg):
                    i2, r4 = (cc // 4) % 2, cc % 4
                    if hg == 0:
                        sc.w("t", s_dv, dv_m[("y1", cc, 1)])
                        base_dh = 1024
                    else:
                        if cc + 1 <= 31:
                            sc.w("t", s_dv, dv_m[("xmod", cc + 1, 1)])
                        sc.w("t", s_dv, dv_m[("state", cc, 0)])
                        base_dh = 3072

                    def f(t, cc=cc, i2=i2, r4=r4, hg=hg, base_dh=base_dh):
                        for h2 in range(2):
                            _li = t.matmul(
                                psum[0:64, base_dh + h2 * 512:
                                     base_dh + (h2 + 1) * 512],
                                B_tm4[i2][:, r4 * 64:(r4 + 1) * 64],
                                xmod[cc % 2][:, hg * 1024 + h2 * 512:
                                             hg * 1024 + (h2 + 1) * 512],
                                start=True, stop=True)
                        _li.then_inc(s_pe.h, 1)
                    sc.em("t", f)
                    s_pe.v += 1
                    pe_m[("dh", cc, hg)] = s_pe.v

                def emit_state(cc, hg):
                    i2, r4 = (cc // 4) % 2, cc % 4
                    sc.w("v", s_pe, pe_m[("dh", cc, hg)])

                    def f(v, i2=i2, r4=r4, hg=hg):
                        dhp = psum[0:64, 1024:2048] if hg == 0 else \
                            psum[0:64, 3072:4096]
                        v.tensor_tensor(stf[hg][:], stf[hg][:], dhp,
                                        ALU.add)
                        e3 = ebc4[i2][:, r4 * 32 + hg * 16:
                                      r4 * 32 + (hg + 1) * 16]\
                            .unsqueeze(2).broadcast_to((64, 16, 64))
                        s3 = stf[hg][:].rearrange("p (h e) -> p h e", e=64)
                        o3 = stf[hg][:].rearrange("p (h e) -> p h e", e=64)
                        _li = v.tensor_tensor(o3, s3, e3, ALU.mult)
                        _li.then_inc(s_dv.h, 1)
                    sc.em("v", f)
                    s_dv.v += 1
                    dv_m[("state", cc, hg)] = s_dv.v

                def emit_sttb(cc, hg):
                    sc.w("g", s_dv, dv_m[("state", cc, hg)])
                    sc.em("g", lambda g, hg=hg: g.tensor_copy(
                        stb[hg][:], stf[hg][:]).then_inc(s_gp.h, 1))
                    s_gp.v += 1
                    gp_m[("sttb", cc, hg)] = s_gp.v

                def emit_y2(cc, hg):
                    if hg == 0 and cc >= 1:
                        sc.w("v", s_ac, ac_m[("y3", cc - 1)])

                    def f(v, cc=cc, hg=hg):
                        b = hg * 1024
                        v.tensor_tensor(y2b[:, b:b + 1024],
                                        y2b[:, b:b + 1024],
                                        xD[cc % 2][:, b:b + 1024], ALU.add)
                        v.bn_stats(ybn[:, hg * 12:hg * 12 + 6],
                                   y2b[:, b:b + 512])
                        _li = v.bn_stats(ybn[:, hg * 12 + 6:hg * 12 + 12],
                                         y2b[:, b + 512:b + 1024])
                        _li.then_inc(s_dv.h, 1)
                    sc.em("v", f)
                    s_dv.v += 1
                    dv_m[("y2", cc, hg)] = s_dv.v

                def emit_aggr(cc):
                    FEN()
                    sc.em("v", lambda v: v.bn_aggr(yms[:, 0:2], ybn[:])
                          .then_inc(s_dv.h, 1))
                    s_dv.v += 1

                def emit_stt(cc):
                    FEN()
                    sc.em("v", lambda v: v.scalar_tensor_tensor(
                        yms[:, 2:3], yms[:, 0:1], yms[:, 0:1], yms[:, 1:2],
                        ALU.mult, ALU.add).then_inc(s_dv.h, 1))
                    s_dv.v += 1
                    dv_m[("stt", cc)] = s_dv.v

                def emit_sqrt(cc):
                    sc.w("s", s_dv, dv_m[("stt", cc)])
                    sc.em("s", lambda s: s.activation(
                        yms[:, 3:4], yms[:, 2:3], AF.Sqrt, bias=FP32_EPS)
                        .then_inc(s_ac.h, 1))
                    s_ac.v += 1
                    ac_m[("sqrt", cc)] = s_ac.v

                def emit_recip(cc):
                    sc.w("v", s_ac, ac_m[("sqrt", cc)])
                    sc.em("v", lambda v: v.reciprocal(
                        yms[:, 3:4], yms[:, 3:4]).then_inc(s_dv.h, 1))
                    s_dv.v += 1
                    dv_m[("recip", cc)] = s_dv.v

                def emit_acty3(cc):
                    sc.w("s", s_dv, dv_m[("recip", cc)])
                    sc.em("s", lambda s: s.activation(
                        y3b[:], y2b[:], AF.Copy, scale=yms[:, 3:4])
                        .then_inc(s_ac.h, 1))
                    s_ac.v += 1
                    ac_m[("y3", cc)] = s_ac.v

                def emit_y3T(cc):
                    sc.w("t", s_ac, ac_m[("y3", cc)])
                    sc.w("t", s_dv, dv_m[("state", cc, 0)])

                    def f(t):
                        pbb = pk(1024, 1024).bitcast(BF16)
                        for k in range(16):
                            _li = t.transpose(
                                pbb[:, k * 128:(k + 1) * 128],
                                y3b[:, k * 128:(k + 1) * 128], ident_b[:])
                        _li.then_inc(s_pe.h, 1)
                    sc.em("t", f)
                    s_pe.v += 1
                    pe_m[("y3T", cc)] = s_pe.v

                def emit_gate(cc):
                    i2, r4 = (cc // 4) % 2, cc % 4
                    sc.w("v", s_pe, pe_m[("y3T", cc)])
                    if cc >= 1:
                        sc.w("v", s_pe, pe_m[("op", cc - 1)])
                    sc.w("v", s_ac, ac_m[("silu", cc // 4)])

                    def f(v, i2=i2, r4=r4):
                        pbb = pk(1024, 1024).bitcast(BF16)
                        zsl = z_t[i2][:].rearrange(
                            "p (c t) -> p c t",
                            t=TT)[:, :, r4 * 128:(r4 + 1) * 128]
                        _li = v.tensor_tensor(
                            yg[:].rearrange("p (c t) -> p c t", t=128),
                            pbb[:].rearrange("p (c t) -> p c t", t=128),
                            zsl, ALU.mult)
                        _li.then_inc(s_dv.h, 1)
                    sc.em("v", f)
                    s_dv.v += 1
                    dv_m[("gate", cc)] = s_dv.v

                def emit_op(cc, half):
                    if half == 0:
                        sc.w("t", s_dv, dv_m[("gate", cc)])
                        if cc + 1 <= 31:
                            sc.w("t", s_dv, dv_m[("y1", cc + 1, 0)])
                        else:
                            sc.w("t", s_dv, dv_m[("xm", cc - 1)])

                    def f(t, cc=cc, half=half):
                        for k in range(half * 8, half * 8 + 8):
                            for oh in range(2):
                                _li = t.matmul(
                                    pk(oh * 512),
                                    yg[:, k * 128:(k + 1) * 128],
                                    wo_bf[:, k * D + oh * 512:
                                          k * D + (oh + 1) * 512],
                                    start=(k == 0), stop=(k == 15))
                        if half == 1:
                            _li.then_inc(s_pe.h, 1)
                    sc.em("t", f)
                    if half == 1:
                        s_pe.v += 1
                        pe_m[("op", cc)] = s_pe.v

                def emit_xm(cc):
                    sc.w("v", s_pe, pe_m[("op", cc)])
                    sc.w("v", d_u, ld_u2[cc])
                    if cc >= 2:
                        sc.w("v", d_sc, wr_xm[cc - 2])

                    def f(v, cc=cc):
                        xmc = xm[cc % 2]
                        for oh in range(2):
                            v.tensor_tensor(
                                xmc[:, oh * 512:(oh + 1) * 512],
                                pk(oh * 512),
                                u2[cc % 2][:, oh * 512:(oh + 1) * 512],
                                ALU.add)
                        v.bn_stats(bnx[:, 0:6], xmc[:, 0:512])
                        v.bn_stats(bnx[:, 6:12], xmc[:, 512:1024])
                        v.reduce_max(mmt1[:, cc:cc + 1], xmc[:], AXX)
                        v.tensor_reduce(mmt1[:, 32 + cc:33 + cc], xmc[:],
                                        AXX, ALU.min)
                        _li = v.bn_aggr(mvx[:], bnx[:])
                        _li.then_inc(s_dv.h, 1)
                    sc.em("v", f)
                    s_dv.v += 1
                    dv_m[("xm", cc)] = s_dv.v

                def emit_xm2(cc):
                    if cc + 1 > 31:
                        FEN()

                    def f(v, cc=cc):
                        i2 = (cc // 4) % 2
                        col = i2 * 4 + cc % 4
                        v.tensor_copy(m2_s[:, cc:cc + 1], mvx[:, 0:1])
                        _li = v.tensor_scalar(
                            v2c[:, col:col + 1], mvx[:, 1:2], LN_EPS, None,
                            ALU.add)
                        _li.then_inc(s_dv.h, 1)
                    sc.em("v", f)
                    s_dv.v += 1
                    dv_m[("xm2", cc)] = s_dv.v

                def emit_dma(cc):
                    sc.w("g", s_dv, dv_m[("xm", cc)])
                    sc.em("g", lambda g, cc=cc: g.dma_start(
                        out=xmid_scr[cc * 128:(cc + 1) * 128, :],
                        in_=xm[cc % 2][:]).then_inc(d_sc.h, 16))
                    d_sc.v += 16
                    wr_xm[cc] = d_sc.v

                def emit_iv_sqrt(tt):
                    sc.w("s", s_dv, dv_m[("xm2", 4 * tt + 3)])
                    if tt >= 1:
                        sc.w("s", s_dv, dv_m[("ivr", tt - 1)])
                    sc.em("s", lambda s, tt=tt: s.activation(
                        iv4[:], v2c[:, (tt % 2) * 4:(tt % 2) * 4 + 4],
                        AF.Sqrt).then_inc(s_ac.h, 1))
                    s_ac.v += 1
                    ac_m[("ivs", tt)] = s_ac.v

                def emit_iv_rec(tt):
                    sc.w("v", s_ac, ac_m[("ivs", tt)])
                    sc.em("v", lambda v, tt=tt: v.reciprocal(
                        inv2_s[:, 4 * tt:4 * tt + 4], iv4[:])
                        .then_inc(s_dv.h, 1))
                    s_dv.v += 1
                    dv_m[("ivr", tt)] = s_dv.v

                emit_loads(0)
                emit_clo(0)
                emit_dts(0)
                emit_prep(0)
                emit_smalls_tr(0)
                emit_smalls_evac(0)
                emit_sMM(0)
                emit_s4evac(0)
                emit_es4(0)
                emit_expsl(0)
                emit_scol4(0)
                emit_cbt4(0)
                emit_esltr(0)
                emit_mt4(0)
                emit_eslr(0)
                emit_ebcmm(0)
                emit_ebc4(0)
                emit_silu(0)
                emit_xt(0, 0)
                emit_xt(0, 1)
                emit_xmod(0, 0)
                emit_xmod(0, 1)

                for c in range(36):
                    i, r = c // 4, c % 4
                    j = i + 1
                    if 1 <= c <= 32:
                        emit_y3T(c - 1)
                    if 2 <= c <= 33:
                        emit_xm(c - 2)
                    if 1 <= c <= 32:
                        emit_gate(c - 1)
                    if 2 <= c <= 33:
                        emit_xm2(c - 2)
                        emit_dma(c - 2)
                    if r == 0 and 1 <= i <= 7:
                        emit_ebc4(i)
                    if r == 0 and 1 <= j <= 7:
                        emit_loads(j)
                        emit_clo(j)
                    if c <= 31:
                        emit_u2(c)
                    if c + 1 <= 31:
                        emit_xt(c + 1, 0)
                        emit_xt(c + 1, 1)
                    if r == 1 and j <= 7:
                        emit_dts(j)
                        emit_prep(j)
                    if r == 2 and j <= 7:
                        emit_smalls_tr(j)
                    if c <= 31:
                        emit_y(c, 0)
                        emit_y(c, 1)
                        emit_y1(c, 0)
                    if c + 1 <= 31:
                        emit_xmod(c + 1, 0)
                    if r == 2 and j <= 7:
                        emit_smalls_evac(j)
                        emit_sMM(j)
                    if c <= 31:
                        emit_y1(c, 1)
                        emit_dh(c, 0)
                        emit_state(c, 0)
                    if r == 2 and j <= 7:
                        emit_s4evac(j)
                        emit_es4(j)
                        emit_expsl(j)
                    if r == 2 and 1 <= i <= 8:
                        emit_iv_sqrt(i - 1)
                    if 1 <= c <= 32:
                        emit_op(c - 1, 0)
                    if c <= 31:
                        emit_y2(c, 0)
                    if c + 1 <= 31:
                        emit_xmod(c + 1, 1)
                    if c <= 31:
                        emit_y2(c, 1)
                    if r == 2 and j <= 7:
                        emit_scol4(j)
                    if r == 2 and 1 <= i <= 8:
                        emit_iv_rec(i - 1)
                    if c <= 31:
                        emit_aggr(c)
                        emit_stt(c)
                        emit_sqrt(c)
                        emit_dh(c, 1)
                    if 1 <= c <= 32:
                        emit_op(c - 1, 1)
                    if c <= 31:
                        emit_recip(c)
                        emit_state(c, 1)
                        emit_acty3(c)
                        emit_sttb(c, 0)
                        emit_sttb(c, 1)
                    if r == 1 and j <= 7:
                        emit_silu(j)
                    if r == 3 and j <= 7:
                        emit_cbt4(j)
                        emit_esltr(j)
                        emit_mt4(j)
                        emit_eslr(j)
                        emit_ebcmm(j)

                sc.w("v", s_dv, dv_m[("ivr", 7)])

                def v_mm1a(v):
                    v.tensor_tensor(mmt1[:, 0:32], mmt1[:, 0:32], m2_s[:],
                                    ALU.subtract)
                    v.tensor_tensor(mmt1[:, 32:64], mmt1[:, 32:64], m2_s[:],
                                    ALU.subtract).then_inc(s_dv.h, 1)
                sc.em("v", v_mm1a)
                s_dv.v += 1
                sc.w("v", s_dv, s_dv.v)

                def v_mm1b(v):
                    v.tensor_tensor(mmt1[:, 0:32], mmt1[:, 0:32], inv2_s[:],
                                    ALU.mult)
                    v.tensor_tensor(mmt1[:, 32:64], mmt1[:, 32:64],
                                    inv2_s[:], ALU.mult).then_inc(s_dv.h, 1)
                sc.em("v", v_mm1b)
                s_dv.v += 1
                sc.w("v", s_dv, s_dv.v)

                def v_mm1c(v):
                    v.reduce_max(mmt1[:, 0:1], mmt1[:, 0:32], AXX)
                    v.tensor_reduce(mmt1[:, 1:2], mmt1[:, 32:64], AXX,
                                    ALU.min).then_inc(s_dv.h, 1)
                sc.em("v", v_mm1c)
                s_dv.v += 1
                sc.w("v", s_dv, s_dv.v)
                sc.em("v", lambda v: v.tensor_scalar(
                    mmt1[:, 1:2], mmt1[:, 1:2], -1.0, None, ALU.mult)
                    .then_inc(s_dv.h, 1))
                s_dv.v += 1
                sc.w("t", s_dv, s_dv.v)

                def t_trm1(t):
                    t.transpose(psum[0:1, 0:128], mmt1[:, 0:1], identf[:])
                    _li = t.transpose(psum[0:1, 128:256], mmt1[:, 1:2],
                                      identf[:])
                    _li.then_inc(s_pe.h, 1)
                sc.em("t", t_trm1)
                s_pe.v += 1
                sc.w("v", s_pe, s_pe.v)

                def v_mm2(v):
                    v.reduce_max(mmrow[:, 0:1], psum[0:1, 0:128], AXX)
                    _li = v.reduce_max(mmrow[:, 1:2], psum[0:1, 128:256],
                                       AXX)
                    _li.then_inc(s_dv.h, 1)
                sc.em("v", v_mm2)
                s_dv.v += 1
                barrier(sc)
                run_phase(sc)
            ms.close()

        def collective(sc, in_t, out_t, after_fn):
            sc.em("g", lambda g: g.dma_start(out=in_t.ap(), in_=mmrow[:])
                  .then_inc(d_cc.h, 16))
            d_cc.v += 16
            sc.w("g", d_cc, d_cc.v)
            sc.em("g", lambda g: g.collective_compute(
                "AllReduce", ALU.max,
                replica_groups=[list(range(len(in_maps)))],
                ins=[in_t.ap().opt()],
                outs=[out_t.ap().opt()]).then_inc(s_gp.h, 1))
            s_gp.v += 1
            sc.w("g", s_gp, s_gp.v)
            sc.em("g", lambda g: g.dma_start(out=ccr[:], in_=out_t.ap())
                  .then_inc(d_cc.h, 16))
            d_cc.v += 16
            sc.w("v", d_cc, d_cc.v)
            after_fn(sc)

        if upto in ("P2", "P3"):
            sc = Sched()

            def after1(sc):
                def v_f1(v):
                    v.tensor_tensor(sc_s[:, 8:9], ccr[:, 0:1], ccr[:, 1:2],
                                    ALU.add).then_inc(s_dv.h, 1)
                sc.em("v", v_f1)
                s_dv.v += 1
                sc.w("v", s_dv, s_dv.v)
                sc.em("v", lambda v: v.tensor_scalar(
                    sc_s[:, 9:10], sc_s[:, 8:9], 1.0 / 255.0, 1e-8,
                    ALU.mult, ALU.max).then_inc(s_dv.h, 1))
                s_dv.v += 1
                sc.w("v", s_dv, s_dv.v)
                sc.em("v", lambda v: v.reciprocal(
                    sc_s[:, 10:11], sc_s[:, 9:10]).then_inc(s_dv.h, 1))
                s_dv.v += 1
                sc.w("v", s_dv, s_dv.v)
                sc.em("v", lambda v: v.tensor_tensor(
                    sc_s[:, 11:12], ccr[:, 1:2], sc_s[:, 10:11],
                    ALU.mult).then_inc(s_dv.h, 1))
                s_dv.v += 1
                iscr = sb("cc_i32", [1, 2], I32)
                sc.w("v", s_dv, s_dv.v)

                sc.em("v", lambda v: v.tensor_copy(
                    iscr[:, 0:1], sc_s[:, 11:12]).then_inc(s_dv.h, 1))
                s_dv.v += 1
                sc.w("v", s_dv, s_dv.v)
                sc.em("v", lambda v: v.tensor_copy(
                    sc_s[:, 11:12], iscr[:, 0:1]).then_inc(s_dv.h, 1))
                s_dv.v += 1
                sc.w("v", s_dv, s_dv.v)
                sc.em("v", lambda v: v.tensor_scalar(
                    sc_s[:, 12:13], sc_s[:, 11:12], -128.0, None,
                    ALU.add).then_inc(s_dv.h, 1))
                s_dv.v += 1
                sc.w("v", s_dv, s_dv.v)
                sc.em("v", lambda v: v.tensor_scalar(
                    sc_s[:, 12:13], sc_s[:, 12:13], -128.0, 127.0,
                    ALU.max, ALU.min).then_inc(s_dv.h, 1))
                s_dv.v += 1
                sc.w("v", s_dv, s_dv.v)
                sc.em("v", lambda v: v.tensor_tensor(
                    sc_s[:, 13:14], sc_s[:, 9:10], sc_s[0:1, 2:3],
                    ALU.mult).then_inc(s_dv.h, 1))
                s_dv.v += 1
                sc.w("v", s_dv, s_dv.v)
                sc.em("v", lambda v: v.tensor_tensor(
                    sc_s[:, 14:15], sc_s[:, 13:14], sc_s[:, 12:13],
                    ALU.mult).then_inc(s_dv.h, 1))
                s_dv.v += 1
                sc.w("v", s_dv, s_dv.v)
                sc.em("v", lambda v: v.tensor_scalar(
                    sc_s[:, 14:15], sc_s[:, 14:15], -1.0, None,
                    ALU.mult).then_inc(s_dv.h, 1))
                s_dv.v += 1
                sc.w("t", s_dv, s_dv.v)
                sc.em("t", lambda t: t.matmul(
                    psum[:, 0:5], ones_f[0:1, :], sc_s[0:1, 10:15],
                    start=True, stop=True).then_inc(s_pe.h, 1))
                s_pe.v += 1
                sc.w("v", s_pe, s_pe.v)

                def v_cols(v):
                    v.tensor_copy(eff_s[:, 4:5], psum[:, 0:1])
                    v.tensor_copy(eff_s[:, 5:6], psum[:, 2:3])
                    v.tensor_copy(eff_s[:, 0:1], psum[:, 3:4])
                    v.tensor_copy(eff_s[:, 6:7], psum[:, 4:5])\
                        .then_inc(s_dv.h, 1)
                sc.em("v", v_cols)
                s_dv.v += 1
                sc.w("v", s_dv, s_dv.v)
                sc.em("v", lambda v: v.tensor_scalar(
                    q1s_s[:], inv2_s[:], eff_s[:, 4:5], None,
                    ALU.mult).then_inc(s_dv.h, 1))
                s_dv.v += 1
                sc.w("v", s_dv, s_dv.v)
                sc.em("v", lambda v: v.tensor_tensor(
                    q1b_s[:], m2_s[:], q1s_s[:], ALU.mult).then_inc(s_dv.h, 1))
                s_dv.v += 1
                sc.w("v", s_dv, s_dv.v)
                sc.em("v", lambda v: v.tensor_scalar(
                    q1b_s[:], q1b_s[:], -1.0, eff_s[:, 5:6],
                    ALU.mult, ALU.add).then_inc(s_dv.h, 1))
                s_dv.v += 1
                sc.w("v", s_dv, s_dv.v)
                sc.em("v", lambda v: v.scalar_tensor_tensor(
                    b1mod_s[:], colsum1_s[:], eff_s[:, 6:7], b1col_s[:],
                    ALU.mult, ALU.add).then_inc(s_dv.h, 1))
                s_dv.v += 1
            collective(sc, cc1_in, cc1_out, after1)
            barrier(sc)
            run_phase(sc)

        if upto in ("P2", "P3"):
            with ExitStack() as ps:
                xq1_fm = sb("xq1_fm", [128, 8 * T], BF16, ps)
                xm2 = [sb(f"xm2_{k}", [128, 4 * D], F32, ps)
                       for k in range(2)]
                qtmp = [sb(f"qtmp{k}", [128, D], F32, ps)
                        for k in range(2)]
                qint = sb("qint", [128, D], I32, ps)
                xqtm = [sb(f"xqtm{k}", [128, D], BF16, ps)
                        for k in range(2)]
                w1r = [sb(f"w1r{k}", [128, 1024], BF16, ps)
                       for k in range(2)]
                hrow = [sb(f"hrow{k}", [128, HF], F32, ps)
                        for k in range(2)]

                sc = Sched()
                ldm = {}
                acq = {}
                dvqx = {}
                dvq2 = {}
                petr = {}
                for i in range(NTT):
                    if i >= 2:
                        sc.w("y", s_dv, dvq2[4 * (i - 2) + 3])
                    sc.em("y", lambda y, i=i: y.dma_start(
                        out=xm2[i % 2][:].rearrange(
                            "p (c d) -> p c d", d=D),
                        in_=xmid_scr[i * TT:(i + 1) * TT, :].rearrange(
                            "(c p) d -> p c d", p=128))
                        .then_inc(d_u.h, 16))
                    d_u.v += 16
                    ldm[i] = d_u.v
                    for r in range(4):
                        c = 4 * i + r
                        sc.w("s", d_u, ldm[i])
                        if c >= 2:
                            sc.w("s", s_dv, dvqx[c - 2])
                        sc.em("s", lambda s, i=i, r=r, c=c: s.activation(
                            qtmp[c % 2][:], xm2[i % 2][:, r * D:(r + 1) * D],
                            AF.Identity, scale=q1s_s[:, c:c + 1],
                            bias=q1b_s[:, c:c + 1]).then_inc(s_ac.h, 1))
                        s_ac.v += 1
                        acq[c] = s_ac.v
                        sc.w("v", s_ac, acq[c])
                        if c >= 2:
                            sc.w("v", s_pe, petr[c - 2])

                        def v_qx(v, c=c):
                            v.tensor_copy(qint[:], qtmp[c % 2][:])
                            _li = v.tensor_scalar(xqtm[c % 2][:], qint[:],
                                                  clip_lo[:], clip_hi[:],
                                                  ALU.max, ALU.min)
                            _li.then_inc(s_dv.h, 1)
                        sc.em("v", v_qx)
                        s_dv.v += 1
                        dvqx[c] = s_dv.v
                        sc.w("t", s_dv, dvqx[c])
                        if c >= 2:
                            sc.w("t", s_dv, dvq2[c - 2])

                        def t_q(t, c=c):
                            pbb = pk(2048 + (c % 2) * 512).bitcast(BF16)
                            for dd in range(8):
                                _li = t.transpose(
                                    pbb[:, dd * 128:(dd + 1) * 128],
                                    xqtm[c % 2][:, dd * 128:(dd + 1) * 128],
                                    ident_b[:])
                            _li.then_inc(s_pe.h, 1)
                        sc.em("t", t_q)
                        s_pe.v += 1
                        petr[c] = s_pe.v
                        sc.w("v", s_pe, petr[c])

                        def v_qev(v, c=c):
                            pbb = pk(2048 + (c % 2) * 512).bitcast(BF16)
                            dst = xq1_fm[:].rearrange(
                                "p (d t) -> p d t",
                                t=T)[:, :, c * 128:(c + 1) * 128]
                            _li = v.tensor_copy(dst, pbb[:].rearrange(
                                "p (d t) -> p d t", t=128))
                            _li.then_inc(s_dv.h, 1)
                        sc.em("v", v_qev)
                        s_dv.v += 1
                        dvq2[c] = s_dv.v

                wld = {}
                ach = {}
                acseg = {}
                schw = {}
                dvhm = {}
                for j in range(32):
                    if j >= 2:
                        sc.w("y", s_ac, ach[j - 2])
                    sc.em("y", lambda y, j=j: y.dma_start(
                        out=w1r[j % 2][:],
                        in_=w1q_scr[j].rearrange("p d c -> p (d c)"))
                        .then_inc(d_w.h, 16))
                    d_w.v += 16
                    wld[j] = d_w.v
                    sc.w("t", d_w, wld[j])
                    if j == 0:
                        sc.w("t", s_dv, dvq2[31])
                    if j >= 2:
                        sc.w("s", d_sc, schw[j - 2])
                        sc.w("s", s_dv, dvhm[j - 2])
                    for ts_ in range(8):
                        gseg = 8 * j + ts_
                        if gseg >= 2:
                            sc.w("t", s_ac, acseg[gseg - 2])

                        def t_f1(t, j=j, ts_=ts_):
                            pb = pk((ts_ % 2) * 512)
                            for dd in range(8):
                                _li = t.matmul(
                                    pb,
                                    w1r[j % 2][:, dd * 128:(dd + 1) * 128],
                                    xq1_fm[:, dd * T + ts_ * 512:
                                           dd * T + (ts_ + 1) * 512],
                                    start=(dd == 0), stop=(dd == 7))
                            _li.then_inc(s_pe.h, 1)
                        sc.em("t", t_f1)
                        s_pe.v += 1
                        sc.w("s", s_pe, s_pe.v)
                        sc.em("s", lambda s, j=j, ts_=ts_: s.activation(
                            hrow[j % 2][:, ts_ * 512:(ts_ + 1) * 512],
                            pk((ts_ % 2) * 512), AF.Gelu,
                            scale=eff_s[:, 0:1],
                            bias=b1mod_s[:, j:j + 1]).then_inc(s_ac.h, 1))
                        s_ac.v += 1
                        acseg[gseg] = s_ac.v
                    ach[j] = s_ac.v
                    sc.w("v", s_ac, ach[j])

                    def v_hmm(v, j=j):
                        v.reduce_max(mmt2[:, j:j + 1], hrow[j % 2][:], AXX)
                        _li = v.tensor_reduce(mmt2[:, 32 + j:33 + j],
                                        hrow[j % 2][:], AXX, ALU.min)
                        _li.then_inc(s_dv.h, 1)
                    sc.em("v", v_hmm)
                    s_dv.v += 1
                    dvhm[j] = s_dv.v
                    sc.w("g", s_ac, ach[j])
                    sc.em("g", lambda g, j=j: g.dma_start(
                        out=h_scr[j], in_=hrow[j % 2][:])
                        .then_inc(d_sc.h, 16))
                    d_sc.v += 16
                    schw[j] = d_sc.v

                sc.w("v", s_dv, s_dv.v)

                def v_hm(v):
                    v.reduce_max(mmt2[:, 0:1], mmt2[:, 0:32], AXX)
                    v.tensor_reduce(mmt2[:, 1:2], mmt2[:, 32:64], AXX,
                                    ALU.min).then_inc(s_dv.h, 1)
                sc.em("v", v_hm)
                s_dv.v += 1
                sc.w("v", s_dv, s_dv.v)
                sc.em("v", lambda v: v.tensor_scalar(
                    mmt2[:, 1:2], mmt2[:, 1:2], -1.0, None, ALU.mult)
                    .then_inc(s_dv.h, 1))
                s_dv.v += 1
                sc.w("t", s_dv, s_dv.v)
                def t_trm2(t):
                    t.transpose(psum[0:1, 0:128], mmt2[:, 0:1], identf[:])
                    _li = t.transpose(psum[0:1, 128:256], mmt2[:, 1:2], identf[:])
                    _li.then_inc(s_pe.h, 1)
                sc.em("t", t_trm2)
                s_pe.v += 1
                sc.w("v", s_pe, s_pe.v)

                def v_hm2(v):
                    v.reduce_max(mmrow[:, 0:1], psum[0:1, 0:128], AXX)
                    _li = v.reduce_max(mmrow[:, 1:2], psum[0:1, 128:256], AXX)
                    _li.then_inc(s_dv.h, 1)
                sc.em("v", v_hm2)
                s_dv.v += 1
                barrier(sc)
                run_phase(sc)

        if upto == "P3":
            sc = Sched()

            def after2(sc):
                sc.em("v", lambda v: v.tensor_tensor(
                    sc_s[:, 16:17], ccr[:, 0:1], ccr[:, 1:2],
                    ALU.add).then_inc(s_dv.h, 1))
                s_dv.v += 1
                sc.w("v", s_dv, s_dv.v)
                sc.em("v", lambda v: v.tensor_scalar(
                    sc_s[:, 17:18], sc_s[:, 16:17], 1.0 / 255.0, 1e-8,
                    ALU.mult, ALU.max).then_inc(s_dv.h, 1))
                s_dv.v += 1
                sc.w("v", s_dv, s_dv.v)
                sc.em("v", lambda v: v.reciprocal(
                    sc_s[:, 18:19], sc_s[:, 17:18]).then_inc(s_dv.h, 1))
                s_dv.v += 1
                sc.w("v", s_dv, s_dv.v)
                sc.em("v", lambda v: v.tensor_tensor(
                    sc_s[:, 19:20], ccr[:, 1:2], sc_s[:, 18:19],
                    ALU.mult).then_inc(s_dv.h, 1))
                s_dv.v += 1
                iscr2 = sb("cc_i322", [1, 2], I32)
                sc.w("v", s_dv, s_dv.v)

                sc.em("v", lambda v: v.tensor_copy(iscr2[:, 0:1], sc_s[:, 19:20]).then_inc(s_dv.h, 1))
                s_dv.v += 1
                sc.w("v", s_dv, s_dv.v)
                sc.em("v", lambda v: v.tensor_copy(sc_s[:, 19:20], iscr2[:, 0:1]).then_inc(s_dv.h, 1))
                s_dv.v += 1
                sc.w("v", s_dv, s_dv.v)
                sc.em("v", lambda v: v.tensor_scalar(sc_s[:, 20:21], sc_s[:, 19:20], -128.0,
                    None, ALU.add).then_inc(s_dv.h, 1))
                s_dv.v += 1
                sc.w("v", s_dv, s_dv.v)
                sc.em("v", lambda v: v.tensor_scalar(sc_s[:, 20:21], sc_s[:, 20:21], -128.0,
                    127.0, ALU.max, ALU.min).then_inc(s_dv.h, 1))
                s_dv.v += 1
                sc.w("v", s_dv, s_dv.v)
                sc.em("v", lambda v: v.tensor_tensor(sc_s[:, 21:22], sc_s[:, 17:18],
                    sc_s[0:1, 3:4], ALU.mult).then_inc(s_dv.h, 1))
                s_dv.v += 1
                sc.w("v", s_dv, s_dv.v)
                sc.em("v", lambda v: v.tensor_tensor(sc_s[:, 23:24], sc_s[:, 20:21],
                    sc_s[:, 21:22], ALU.mult).then_inc(s_dv.h, 1))
                s_dv.v += 1
                sc.w("v", s_dv, s_dv.v)
                sc.em("v", lambda v: v.tensor_scalar(sc_s[:, 23:24], sc_s[:, 23:24], -1.0,
                    None, ALU.mult).then_inc(s_dv.h, 1))
                s_dv.v += 1
                sc.w("v", s_dv, s_dv.v)
                sc.em("v", lambda v: v.scalar_tensor_tensor(b2mod_s[:], colsum2_s[:],
                    sc_s[0:1, 23:24], b2row_s[:],
                    ALU.mult, ALU.add).then_inc(s_dv.h, 1))
                s_dv.v += 1
                sc.w("t", s_dv, s_dv.v)
                sc.em("t", lambda t: t.matmul(
                    psum[:, 0:4], ones_f[0:1, :], sc_s[0:1, 18:22],
                    start=True, stop=True).then_inc(s_pe.h, 1))
                s_pe.v += 1
                sc.w("v", s_pe, s_pe.v)

                def v_cols(v):
                    v.tensor_copy(eff_s[:, 2:3], psum[:, 0:1])
                    v.tensor_copy(eff_s[:, 3:4], psum[:, 2:3])
                    _li = v.tensor_copy(eff_s[:, 1:2], psum[:, 3:4])
                    _li.then_inc(s_dv.h, 1)
                sc.em("v", v_cols)
                s_dv.v += 1
                sc.w("t", s_dv, s_dv.v)
                def t_b2bc(t):
                    for oh in range(2):
                        _li = t.matmul(pk(512 + oh * 512),
                                       ones_f[0:1, :],
                                       b2mod_s[:, oh * 512:(oh + 1) * 512],
                                       start=True, stop=True)
                    _li.then_inc(s_pe.h, 1)
                sc.em("t", t_b2bc)
                s_pe.v += 1
                sc.w("v", s_pe, s_pe.v)
                sc.em("v", lambda v: v.tensor_copy(
                    b2bc[:], psum[:, 512:1536]).then_inc(s_dv.h, 1))
                s_dv.v += 1
            collective(sc, cc2_in, cc2_out, after2)
            barrier(sc)
            run_phase(sc)

        if upto == "P3":
            with ExitStack() as ps:
                w2sb = sb("w2sb", [128, 32 * D], BF16, ps)
                hgrp = [sb(f"hgrp{k}", [128, 32 * 128], F32, ps)
                        for k in range(3)]
                q3t = sb("q3t", [128, 32 * 128], F32, ps)
                q3i = sb("q3i", [128, 32 * 128], I32, ps)
                hq = [sb(f"hq{k}", [128, 32 * 128], BF16, ps)
                      for k in range(2)]
                xm3 = [sb(f"xm3_{k}", [128, D], F32, ps)
                       for k in range(2)]
                ob = [sb(f"ob{k}", [128, D], F32, ps) for k in range(2)]

                sc = Sched()
                for k in range(32):
                    sc.em("y", lambda y, k=k: y.dma_start(
                        out=w2sb[:, k * D:(k + 1) * D], in_=w2q_scr[k])
                        .then_inc(d_w.h, 16))
                    d_w.v += 16
                w2ld = d_w.v
                sc.w("t", d_w, w2ld)

                ldh = {}
                ldx = {}
                dvh = {}
                obm = {}
                outm = {}
                qtm = {}

                def emit_ldh(cc):
                    if cc >= 3:
                        sc.w("y", s_ac, qtm[cc - 3])
                    for part in range(16):
                        sc.em("y", lambda y, cc=cc, part=part: y.dma_start(
                            out=hgrp[cc % 3][:].rearrange(
                                "p (j t) -> p j t",
                                t=128)[:, part * 2:(part + 1) * 2],
                            in_=h_scr[part * 2:(part + 1) * 2, :,
                                      cc * 128:(cc + 1) * 128]
                            .transpose([1, 0, 2]))
                            .then_inc(d_u.h, 16))
                        d_u.v += 16
                    ldh[cc] = d_u.v

                emit_ldh(0)
                emit_ldh(1)
                for c in range(32):
                    if c + 2 <= 31:
                        emit_ldh(c + 2)
                    if c >= 2:
                        sc.w("y", s_dv, outm[c - 2])
                    sc.em("y", lambda y, c=c: y.dma_start(
                        out=xm3[c % 2][:],
                        in_=xmid_scr[c * 128:(c + 1) * 128, :])
                        .then_inc(d_u.h, 16))
                    d_u.v += 16
                    ldx[c] = d_u.v

                    sc.w("s", d_u, ldh[c])
                    if c >= 1:
                        sc.w("s", s_dv, dvh[c - 1])
                    sc.em("s", lambda s, c=c: s.activation(
                        q3t[:], hgrp[c % 3][:], AF.Identity,
                        scale=eff_s[:, 2:3], bias=eff_s[:, 3:4])
                        .then_inc(s_ac.h, 1))
                    s_ac.v += 1
                    qtm[c] = s_ac.v
                    sc.w("v", s_ac, s_ac.v)
                    sc.w("v", d_u, ldx[c])
                    if c >= 2:
                        sc.w("v", s_pe, obm[c - 2])

                    def v_q3(v, c=c):
                        v.tensor_copy(q3i[:], q3t[:])
                        v.tensor_scalar(hq[c % 2][:], q3i[:],
                                        clip_lo[:], clip_hi[:],
                                        ALU.max, ALU.min)
                        _li = v.tensor_tensor(xm3[c % 2][:], xm3[c % 2][:],
                                        b2bc[:], ALU.add)
                        _li.then_inc(s_dv.h, 1)
                    sc.em("v", v_q3)
                    s_dv.v += 1
                    dvh[c] = s_dv.v

                    sc.w("t", s_dv, dvh[c])
                    if c >= 2:
                        sc.w("t", s_dv, outm[c - 2])

                    def t_f2(t, c=c):
                        for oh in range(2):
                            pb = pk((c % 2) * 1024 + oh * 512)
                            for k in range(32):
                                _li = t.matmul(
                                    pb,
                                    hq[c % 2][:, k * 128:(k + 1) * 128],
                                    w2sb[:, k * D + oh * 512:
                                         k * D + (oh + 1) * 512],
                                    start=(k == 0), stop=(k == 31))
                        _li.then_inc(s_pe.h, 1)
                    sc.em("t", t_f2)
                    s_pe.v += 1
                    obm[c] = s_pe.v

                    def emit_vout(cc):
                        sc.w("v", s_pe, obm[cc])
                        if cc >= 2:
                            sc.w("v", d_out, 16 * (cc - 1))

                        def v_out(v, cc=cc):
                            for oh in range(2):
                                _li = v.scalar_tensor_tensor(
                                    ob[cc % 2][:, oh * 512:(oh + 1) * 512],
                                    pk((cc % 2) * 1024 + oh * 512),
                                    eff_s[:, 1:2],
                                    xm3[cc % 2][:, oh * 512:(oh + 1) * 512],
                                    ALU.mult, ALU.add)
                            _li.then_inc(s_dv.h, 1)
                        sc.em("v", v_out)
                        s_dv.v += 1
                        outm[cc] = s_dv.v
                        sc.w("g", s_dv, outm[cc])
                        sc.em("g", lambda g, cc=cc: g.dma_start(
                            out=out_d[cc * 128:(cc + 1) * 128, :],
                            in_=ob[cc % 2][:]).then_inc(d_out.h, 16))
                        d_out.v += 16

                    if c >= 1:
                        emit_vout(c - 1)
                    if c == 31:
                        emit_vout(31)
                barrier(sc)
                run_phase(sc)

        sc = Sched()
        sc.w("g", d_sc, d_sc.v)
        sc.w("g", d_out, d_out.v)
        run_phase(sc)

        res = run_bass_kernel_spmd(nc, in_maps,
                                   core_ids=list(range(len(in_maps))))
    return res


def _prep_inputs(inputs):
    u = np.ascontiguousarray(np.asarray(inputs["L0_attn_u"], np.float32))
    ipw = np.asarray(inputs["L0_attn_in_proj_w"], dtype=np.float32)
    winT = np.ascontiguousarray(ipw.T)
    cw = np.asarray(inputs["L0_attn_conv_w"], dtype=np.float32)
    cb = np.asarray(inputs["L0_attn_conv_b"], dtype=np.float32)
    convw = np.zeros((128, 136), np.float32)
    convb = np.zeros((128, 34), np.float32)
    for ct in range(34):
        n = 128 if ct < 33 else 32
        convw[0:n, ct * 4:(ct + 1) * 4] = cw[ct * 128:ct * 128 + n, :]
        convb[0:n, ct] = cb[ct * 128:ct * 128 + n]
    alog = np.ascontiguousarray(
        np.asarray(inputs["L0_attn_A_log"], np.float32).reshape(NH, 1))
    dtb = np.ascontiguousarray(
        np.asarray(inputs["L0_attn_dt_bias"], np.float32).reshape(NH, 1))
    drow = np.ascontiguousarray(
        np.asarray(inputs["L0_attn_Dparam"], np.float32).reshape(1, DI))
    nwcol = np.ascontiguousarray(
        np.asarray(inputs["L0_attn_norm_w"], np.float32).reshape(16, 128).T)
    woT = np.ascontiguousarray(
        np.asarray(inputs["L0_attn_out_proj_w"], np.float32).T)
    w1T = np.ascontiguousarray(np.asarray(inputs["L0_ffn_W1"], np.float32).T)
    b1col = np.ascontiguousarray(
        np.asarray(inputs["L0_ffn_b1"], np.float32).reshape(32, 128).T)
    w2T = np.ascontiguousarray(np.asarray(inputs["L0_ffn_W2"], np.float32).T)
    b2row = np.ascontiguousarray(
        np.asarray(inputs["L0_ffn_b2"], np.float32).reshape(1, D))
    shared = dict(winT=winT, convw=convw, convb=convb, alog=alog, dtb=dtb,
                  drow=drow, nwcol=nwcol, woT=woT, w1T=w1T, b1col=b1col,
                  w2T=w2T, b2row=b2row)
    in_maps = []
    for i in range(u.shape[0]):
        m = dict(shared)
        m["u"] = np.ascontiguousarray(u[i])
        in_maps.append(m)
    return in_maps


def kernel(**inputs):
    in_maps = _prep_inputs(inputs)
    res = build_and_run(in_maps, upto="P3")
    return np.stack([np.asarray(r["out"]) for r in res.results], axis=0)
